# version 21
# baseline (speedup 1.0000x reference)
import numpy as np
import ml_dtypes
from contextlib import ExitStack
import concourse.bass as bass
import concourse.mybir as mybir
from concourse.bass_utils import run_bass_kernel_spmd

F32 = mybir.dt.float32
BF16 = mybir.dt.bfloat16
I32 = mybir.dt.int32
AF = mybir.ActivationFunctionType
ALU = mybir.AluOpType
AX = mybir.AxisListType

D = 2048
SEQ = 2048
H = 16
NCH = 16
NT = 16
NO = 8
NHT = 44
HID = 5632
SCL = float(128.0 ** -0.25)
EPS = 1e-5
RJ = [1, 1, 2, 2, 3, 3, 4, 4]
QORD = [7, 0, 6, 1, 5, 2, 4, 3]
LSS = [0, 1, 42, 43]
RCS = [2, 3, 40, 41]
NEG = -1e30

_SB_GRAN = 256
_PS_GRAN = 2048
_DR_GRAN = 1 << 18


class _Sem:
    __slots__ = ("name", "n", "snaps", "h")

    def __init__(self, name):
        self.name = name
        self.n = 0
        self.snaps = {}
        self.h = None


class _Sched:
    def __init__(self):
        self.streams = {k: [] for k in ("pe", "dve", "act", "pool", "sp")}
        self.know = {k: {} for k in self.streams}
        self.slots = {}
        self.sems = {k: _Sem(k) for k in ("pe", "dve", "act")}
        for i in range(8):
            self.sems[f"dmag{i}"] = _Sem(f"dmag{i}")
            self.sems[f"dmas{i}"] = _Sem(f"dmas{i}")
        self.rot = {"pool": 0, "sp": 0}

    def _keys(self, ap):
        t = ap.tensor
        tn = type(t).__name__
        esz = mybir.dt.size(ap.dtype)
        dims = [list(d) for d in ap.ap]
        name = t.name
        if tn == "DRamTensorHandle":
            span = 1 + sum(s * (c - 1) for s, c in dims)
            lo, hi = ap.offset * esz, (ap.offset + span) * esz
            gran, psum = _DR_GRAN, False
        else:
            psum = tn == "PSumTensorHandle"
            pstride = dims[0][0] if dims else 1
            off = ap.offset % pstride if pstride > 0 else ap.offset
            free = dims[1:]
            span = 1 + sum(s * (c - 1) for s, c in free) if free else 1
            lo, hi = off * esz, (off + span) * esz
            gran = 4 if name == "SM" else (_PS_GRAN if psum else _SB_GRAN)
        return {(name, b) for b in range(lo // gran, (hi - 1) // gran + 1)}, psum

    def op(self, eng, build, reads=(), writes=(), label=""):
        if eng in ("pool", "sp"):
            pfx = "dmag" if eng == "pool" else "dmas"
            sem = self.sems[f"{pfx}{self.rot[eng] % 8}"]
            self.rot[eng] += 1
            amt = 16
        else:
            sem = self.sems[eng]
            amt = 1
        waits = {}
        if amt == 16 and sem.n > 0:
            waits[sem] = sem.n

        def addw(sv):
            if sv is None:
                return
            s, v = sv
            if eng == "pe" and s is self.sems["pe"]:
                return
            if waits.get(s, -1) < v:
                waits[s] = v

        rd, wr = set(), set()
        for ap in reads:
            k, psum = self._keys(ap)
            (wr if psum else rd).update(k)
        for ap in writes:
            k, _ = self._keys(ap)
            wr.update(k)
        rd -= wr
        for k in rd:
            sl = self.slots.get(k)
            if sl:
                addw(sl["w"])
        for k in wr:
            sl = self.slots.get(k)
            if sl:
                addw(sl["w"])
                for r in sl["rs"]:
                    addw(r)
        kn = self.know[eng]
        real = [(s, v) for s, v in waits.items() if kn.get(s, 0) < v]
        sem.n += amt
        val = sem.n
        for s, v in real:
            if kn.get(s, 0) < v:
                kn[s] = v
            snap = s.snaps.get(v)
            if snap:
                for s2, v2 in snap.items():
                    if kn.get(s2, 0) < v2:
                        kn[s2] = v2
        pub = dict(kn)
        pub[sem] = val
        sem.snaps[val] = pub
        for k in wr:
            self.slots[k] = {"w": (sem, val), "rs": []}
        for k in rd:
            sl = self.slots.setdefault(k, {"w": None, "rs": []})
            sl["rs"].append((sem, val))
        self.streams[eng].append((real, build, sem, amt, label))

    def raw(self, eng, build):
        self.streams[eng].append(([], build, None, 0, "raw"))

    def emit(self, block):
        deco = {"pe": block.tensor, "dve": block.vector, "act": block.scalar,
                "pool": block.gpsimd, "sp": block.sync}

        def runner(stream):
            def run(eng):
                for waits, build, sem, amt, label in stream:
                    for s, v in waits:
                        eng.wait_ge(s.h, v)
                    inst = build(eng)
                    if sem is not None:
                        inst.then_inc(sem.h, amt)
            return run

        for name in self.streams:
            deco[name](runner(self.streams[name]))


def build_nc():
    nc = bass.Bass("TRN2", target_bir_lowering=False)

    x_in = nc.dram_tensor("x", [SEQ, D], F32, kind="ExternalInput")
    xo_in = nc.dram_tensor("xown", [1024, D], F32, kind="ExternalInput")
    wq_in = nc.dram_tensor("wqp", [128, 8 * 4096], F32, kind="ExternalInput")
    wk_in = nc.dram_tensor("wkp", [128, 8 * 4096], F32, kind="ExternalInput")
    wv_in = nc.dram_tensor("wvp", [128, 8 * 4096], F32, kind="ExternalInput")
    wo_in = nc.dram_tensor("wop", [128, 16 * 2048], F32, kind="ExternalInput")
    bq_in = nc.dram_tensor("bq_pp", [128, H], F32, kind="ExternalInput")
    bk_in = nc.dram_tensor("bk_pp", [128, H], F32, kind="ExternalInput")
    bv_in = nc.dram_tensor("bv_pp", [128, H], F32, kind="ExternalInput")
    bo_in = nc.dram_tensor("bo_row", [1, D], F32, kind="ExternalInput")
    wg_in = nc.dram_tensor("wgp", [128, 22 * 4096], I32, kind="ExternalInput")
    wu_in = nc.dram_tensor("wup", [128, 22 * 4096], I32, kind="ExternalInput")
    wd_in = nc.dram_tensor("wdp", [128, 44 * 2048], I32, kind="ExternalInput")
    sg_in = nc.dram_tensor("sg_pp", [128, NHT], F32, kind="ExternalInput")
    su_in = nc.dram_tensor("su_pp", [128, NHT], F32, kind="ExternalInput")
    sdb_in = nc.dram_tensor("sdb", [128, D], F32, kind="ExternalInput")
    mask_in = nc.dram_tensor("mask", [128, 768], BF16, kind="ExternalInput")
    id_in = nc.dram_tensor("ident", [128, 128], BF16, kind="ExternalInput")
    y_out = nc.dram_tensor("y", [1024, D], F32, kind="ExternalOutput")

    with ExitStack() as ctx:
        E = ctx.enter_context
        A1 = E(nc.sbuf_tensor("A1", [128, 16384], F32))
        A2 = E(nc.sbuf_tensor("A2", [128, 8192], F32))
        A3 = E(nc.sbuf_tensor("A3", [128, 8192], F32))
        A5 = E(nc.sbuf_tensor("A5", [128, 8192], BF16))
        A6 = E(nc.sbuf_tensor("A6", [128, 8192], BF16))
        A7 = E(nc.sbuf_tensor("A7", [128, 4096], F32))
        WT0 = E(nc.sbuf_tensor("WT0", [128, 2048], BF16))
        WT1 = E(nc.sbuf_tensor("WT1", [128, 2048], BF16))
        H1T = E(nc.sbuf_tensor("H1T", [128, 2048], BF16))
        PSA = E(nc.sbuf_tensor("PSA", [128, 2048], BF16))
        PSB = E(nc.sbuf_tensor("PSB", [128, 1024], BF16))
        PTS = E(nc.sbuf_tensor("PTS", [128, 1024], BF16))
        MSK = E(nc.sbuf_tensor("MSK", [128, 768], BF16))
        IDT = E(nc.sbuf_tensor("IDT", [128, 128], BF16))
        BOR = E(nc.sbuf_tensor("BOR", [1, D], BF16))
        ONE = E(nc.sbuf_tensor("ONE", [1, 128], BF16))
        SGP = E(nc.sbuf_tensor("SGP", [128, NHT], F32))
        SUP = E(nc.sbuf_tensor("SUP", [128, NHT], F32))
        SDB = E(nc.sbuf_tensor("SDB", [128, D], BF16))
        SM = E(nc.sbuf_tensor("SM", [128, 48], F32))
        BNS = E(nc.sbuf_tensor("BNS", [128, 48], F32))
        BQS = E(nc.sbuf_tensor("BQS", [128, H], F32))
        BKS = E(nc.sbuf_tensor("BKS", [128, H], F32))
        BVP = E(nc.sbuf_tensor("BVP", [128, H], F32))
        PS = E(nc.psum_tensor("PS", [128, 4096], F32))

        sch = _Sched()
        for k, s in sch.sems.items():
            s.h = E(nc.semaphore(k))

        a1b = A1.ap().bitcast(BF16)
        a2b = A2.ap().bitcast(BF16)
        a3b = A3.ap().bitcast(BF16)
        a7b = A7.ap().bitcast(BF16)
        psf = PS.ap()
        psb = PS.ap().bitcast(BF16)
        sdbap = SDB.ap()

        def bank(i, n=1):
            return psf[:, i * 512:(i + n) * 512]

        def bankw(i, w):
            return psf[:, i * 512:i * 512 + w]

        def bank16(i):
            return psb[:, i * 1024:(i + 1) * 1024]

        def sm(c):
            return SM.ap()[:, c:c + 1]

        OP = sch.op

        def ld(dst, src, label=""):
            OP("pool", lambda g, d=dst, s=src: g.dma_start(out=d, in_=s),
               reads=[src], writes=[dst], label=label)

        ld(IDT.ap(), id_in.ap(), "ident")
        ld(MSK.ap(), mask_in.ap(), "mask")
        ld(BOR.ap(), bo_in.ap(), "bo")
        ld(SGP.ap(), sg_in.ap(), "sg")
        ld(SUP.ap(), su_in.ap(), "su")
        ld(SDB.ap(), sdb_in.ap(), "sdb")
        ld(BQS.ap(), bq_in.ap(), "bq")
        ld(BKS.ap(), bk_in.ap(), "bk")
        ld(BVP.ap(), bv_in.ap(), "bv")
        OP("dve", lambda v: v.memset(ONE.ap(), 1.0), writes=[ONE.ap()], label="ones")
        OP("dve", lambda v: v.memset(sm(22), EPS), writes=[sm(22)], label="eps")
        OP("dve", lambda v: v.tensor_scalar_mul(BQS.ap(), BQS.ap(), SCL),
           reads=[BQS.ap()], writes=[BQS.ap()], label="bqs")
        OP("dve", lambda v: v.tensor_scalar_mul(BKS.ap(), BKS.ap(), SCL),
           reads=[BKS.ap()], writes=[BKS.ap()], label="bks")

        xrot = [A7.ap()[:, 0:2048], A7.ap()[:, 2048:4096]]

        def wload(dst, w_in, c0, w, label=""):
            src = w_in.ap()[:, c0:c0 + w]
            OP("pool", lambda g, d=dst, s=src: g.dma_start(out=d, in_=s),
               reads=[src], writes=[dst], label=label)

        def xload(src, r, queue="sp"):
            dst = xrot[r]
            OP(queue, lambda g, d=dst, s=src: g.dma_start(out=d, in_=s),
               reads=[src], writes=[dst], label="xld")

        _ln = [0]

        def ln_stats(xap, label):
            r = _ln[0] % 2
            _ln[0] += 1
            bns = BNS.ap()[:, r * 24:(r + 1) * 24]
            for ch in range(4):
                OP("dve", lambda v, o=bns[:, ch * 6:(ch + 1) * 6],
                   x=xap[:, ch * 512:(ch + 1) * 512]: v.bn_stats(o, x),
                   reads=[xap[:, ch * 512:(ch + 1) * 512]],
                   writes=[bns[:, ch * 6:(ch + 1) * 6]], label=label + f"s{ch}")
            c_mv, c_std, c_rstd = 8 + 2 * r, 16 + r, 18 + r
            mv = SM.ap()[:, c_mv:c_mv + 2]
            OP("dve", lambda v, o=mv, s=bns: v.bn_aggr(o, s),
               reads=[bns], writes=[mv], label=label + "a")
            OP("act", lambda a: a.activation(sm(c_std), sm(c_mv + 1), AF.Sqrt,
                                             bias=sm(22)),
               reads=[sm(c_mv + 1), sm(22)], writes=[sm(c_std)], label=label + "d")
            OP("dve", lambda v: v.reciprocal(sm(c_rstd), sm(c_std)),
               reads=[sm(c_std)], writes=[sm(c_rstd)], label=label + "r")
            return r

        def ln_norm(xap, r, label):
            c_mean, c_rstd = 8 + 2 * r, 18 + r
            htile = H1T.ap() if r == 0 else PSA.ap()
            OP("dve", lambda v, x=xap, o=htile: v.tensor_scalar(
                o, x, sm(c_mean), sm(c_rstd), ALU.subtract, ALU.mult),
               reads=[xap, sm(c_mean), sm(c_rstd)], writes=[htile], label=label + "n")
            return htile

        def ln_tail(htile, outFb, cstride, tcol, label):
            for half in range(2):
                bkx = 6 + half

                def tr(t, h=htile, b=bkx, hf=half):
                    last = None
                    for c in range(8):
                        last = t.transpose(
                            bank16(b)[:, c * 128:(c + 1) * 128],
                            h[:, (hf * 8 + c) * 128:(hf * 8 + c + 1) * 128],
                            IDT.ap())
                    return last

                OP("pe", tr, reads=[htile, IDT.ap()], writes=[bank16(bkx)],
                   label=label + f"T{half}")
                seg = outFb[:, half * 8 * cstride:(half * 8 + 8) * cstride]
                dst = seg.rearrange("p (a b) -> p a b", a=8, b=cstride)[:, :, tcol:tcol + 128]
                OP("act", lambda a, b=bkx, d=dst: a.activation(
                    d, bank16(b)[:, 0:1024], AF.Identity),
                   reads=[bank16(bkx)], writes=[dst], label=label + f"E{half}")

        h1F = a1b
        h1oF = a2b
        QF = a3b

        a5f = A5.ap().bitcast(F32)
        xrot4 = [A7.ap()[:, 0:2048], A7.ap()[:, 2048:4096],
                 a5f[:, 0:2048], a5f[:, 2048:4096]]

        def xload4(src, r):
            dst = xrot4[r]
            OP("sp", lambda g, d=dst, s=src: g.dma_start(out=d, in_=s),
               reads=[src], writes=[dst], label="xld")

        wqbuf = [A6.ap()[:, 0:4096], A6.ap()[:, 4096:8192]]

        def emit_q(pr):
            buf = wqbuf[pr % 2]
            wload(buf, wq_in, pr * 4096, 4096, f"wqL{pr}")
            for hp in range(2):
                hh = pr * 2 + hp
                for half in range(2):
                    bkq = half

                    def qmm(t, b=bkq, bu=buf, hp=hp, hf=half):
                        last = None
                        for c in range(NCH):
                            last = t.matmul(
                                bank(b),
                                bu[:, c * 256 + hp * 128:c * 256 + (hp + 1) * 128],
                                h1oF[:, c * 1024 + hf * 512:c * 1024 + (hf + 1) * 512],
                                start=(c == 0), stop=(c == NCH - 1))
                        return last
                    qrd = [buf] + [h1oF[:, c * 1024 + half * 512:c * 1024 + (half + 1) * 512]
                                   for c in range(NCH)]
                    OP("pe", qmm, reads=qrd, writes=[bank(bkq)],
                       label=f"q{hh}_{half}")
                    dst = QF[:, hh * 1024 + half * 512:hh * 1024 + (half + 1) * 512]
                    OP("act", lambda a, b=bkq, d=dst, hh=hh: a.activation(
                        d, bank(b), AF.Identity, bias=BQS.ap()[:, hh:hh + 1],
                        scale=SCL),
                       reads=[bank(bkq), BQS.ap()], writes=[dst], label=f"qe{hh}")

        ln_jobs = [(xo_in.ap()[i * 128:(i + 1) * 128, :], h1oF, 1024, i * 128, f"lo_{i}")
                   for i in range(NO)]
        ln_jobs += [(x_in.ap()[i * 128:(i + 1) * 128, :], h1F, 2048, i * 128, f"l1_{i}")
                    for i in range(NT)]
        xload4(ln_jobs[0][0], 0)
        xload4(ln_jobs[1][0], 1)
        pend = None
        for i, (src, outFb, cs, tc, lab) in enumerate(ln_jobs):
            r = ln_stats(xrot4[i % 4], lab)
            ht = ln_norm(xrot4[i % 4], r, lab)
            if i + 2 < len(ln_jobs):
                xload4(ln_jobs[i + 2][0], (i + 2) % 4)
            if pend is not None:
                ln_tail(*pend)
            pend = (ht, outFb, cs, tc, lab)
            if 9 <= i <= 16:
                emit_q(i - 9)
        ln_tail(*pend)

        wvbuf = A6.ap()[:, 0:4096]
        wkbuf = A6.ap()[:, 4096:8192]
        VTbuf = [A5.ap()[:, 0:4096], a7b[:, 4096:8192]]
        kbuf = [A5.ap()[:, 4096:6144], A5.ap()[:, 6144:8192]]
        PTS4 = [PTS.ap()[:, 0:512], PTS.ap()[:, 512:1024],
                H1T.ap()[:, 512:1024], H1T.ap()[:, 1536:2048]]
        woA = A6.ap()
        woB = a3b
        wload(WT0.ap(), wo_in, 24576, 2048, "woC12")
        wload(WT1.ap(), wo_in, 26624, 2048, "woC13")

        def emit_vmm(prv, kt, bkv):
            vt = VTbuf[prv % 2]

            def vmm(t, b=bkv, kt=kt):
                last = None
                for c in range(NCH):
                    last = t.matmul(
                        bankw(b, 256),
                        h1F[:, c * 2048 + kt * 128:c * 2048 + (kt + 1) * 128],
                        wvbuf[:, c * 256:(c + 1) * 256],
                        start=(c == 0), stop=(c == NCH - 1))
                return last
            vrd = [wvbuf] + [h1F[:, c * 2048 + kt * 128:c * 2048 + (kt + 1) * 128]
                             for c in range(NCH)]
            OP("pe", vmm, reads=vrd, writes=[bank(bkv)],
               label=f"v{prv}_{kt}")
            dst = vt[:, kt * 256:(kt + 1) * 256]
            OP("act", lambda a, b=bkv, d=dst: a.activation(d, bankw(b, 256),
                                                           AF.Identity),
               reads=[bank(bkv)], writes=[dst], label=f"ve{prv}_{kt}")

        for pr in range(8):
            VT = VTbuf[pr % 2]
            if pr == 0:
                wload(wvbuf, wv_in, 0, 4096, "wvL0")
                for kt in range(NT):
                    emit_vmm(0, kt, kt % 4)
            wload(wkbuf, wk_in, pr * 4096, 4096, f"wkL{pr}")
            if pr == 7:
                wload(woA[:, 0:4096], wo_in, 0, 4096, "woA0")

            def kproj_kt(hp, kt):
                hh = pr * 2 + hp
                kb = kbuf[hp]
                if True:
                    def kmm(t, kt=kt, hp=hp):
                        last = None
                        for c in range(NCH):
                            last = t.matmul(
                                bank(6),
                                wkbuf[:, c * 256 + hp * 128:c * 256 + (hp + 1) * 128],
                                h1F[:, c * 2048 + kt * 512:c * 2048 + (kt + 1) * 512],
                                start=(c == 0), stop=(c == NCH - 1))
                        return last
                    krd = [wkbuf] + [h1F[:, c * 2048 + kt * 512:c * 2048 + (kt + 1) * 512]
                                     for c in range(NCH)]
                    OP("pe", kmm, reads=krd, writes=[bank(6)],
                       label=f"k{hh}_{kt}")
                    dst = kb[:, kt * 512:(kt + 1) * 512]
                    OP("act", lambda a, d=dst, hh=hh: a.activation(
                        d, bank(6), AF.Identity, bias=BKS.ap()[:, hh:hh + 1],
                        scale=SCL),
                       reads=[bank(6), BKS.ap()], writes=[dst], label=f"ke{hh}")

            for hp in range(2):
                hh = pr * 2 + hp
                kb = kbuf[hp]
                if hp == 0:
                    for kt in range(4):
                        kproj_kt(0, kt)
                elif pr == 7:
                    wload(woA[:, 4096:8192], wo_in, 4096, 4096, "woA1")
                    wload(A5.ap()[:, 4096:6144], wo_in, 28672, 2048, "woC14")
                else:
                    wload(wvbuf, wv_in, (pr + 1) * 4096, 4096, f"wvL{pr + 1}")

                def s_group(idx, j):
                    R = RJ[j]
                    base = 0 if idx % 2 == 0 else 4
                    moff = 256 if j % 2 == 0 else 0
                    r2 = idx % 2
                    for rr in range(R):
                        lastb = rr == R - 1

                        def smm(t, rr=rr, base=base, j=j, hh=hh, moff=moff,
                                kb=kb, lastb=lastb):
                            last = t.matmul(
                                bank(base + rr),
                                QF[:, hh * 1024 + j * 128:hh * 1024 + (j + 1) * 128],
                                kb[:, rr * 512:(rr + 1) * 512],
                                start=True, stop=not lastb)
                            if lastb:
                                last = t.matmul(bank(base + rr), IDT.ap(),
                                                MSK.ap()[:, moff:moff + 512],
                                                start=False, stop=True)
                            return last
                        OP("pe", smm,
                           reads=[QF[:, hh * 1024 + j * 128:hh * 1024 + (j + 1) * 128],
                                  kb[:, rr * 512:(rr + 1) * 512], IDT.ap(), MSK.ap()],
                           writes=[bank(base + rr)], label=f"s{hh}_{j}_{rr}")
                        OP("dve", lambda v, rr=rr, base=base, r2=r2: v.reduce_max(
                            sm(24 + 4 * r2 + rr), bank(base + rr), axis=AX.X),
                           reads=[bank(base + rr)],
                           writes=[sm(24 + 4 * r2 + rr)], label=f"mxp{hh}_{j}_{rr}")

                def softmax_max(idx, j):
                    R = RJ[j]
                    r2 = idx % 2
                    src = SM.ap()[:, 24 + 4 * r2:24 + 4 * r2 + R]
                    OP("dve", lambda v, sr=src, r2=r2: v.reduce_max(
                        sm(4 + r2), sr, axis=AX.X, negate=True),
                       reads=[src], writes=[sm(4 + r2)], label=f"mx{hh}_{j}")

                def softmax_exp(idx, j):
                    R = RJ[j]
                    base = 0 if idx % 2 == 0 else 4
                    r2 = idx % 2
                    P = PSA.ap() if base == 0 else PSB.ap()
                    for rr in range(R):
                        psv = psf[:, (base + rr) * 512:(base + rr + 1) * 512]
                        Pv = P[:, rr * 512:(rr + 1) * 512]
                        OP("act", lambda a, p=psv, o=Pv, r2=r2, rr=rr: a.activation(
                            o, p, AF.Exp, bias=sm(4 + r2), scale=1.0,
                            accum_out=sm(32 + 4 * r2 + rr)),
                           reads=[psv, sm(4 + r2)],
                           writes=[Pv, sm(32 + 4 * r2 + rr)], label=f"ex{hh}_{j}_{rr}")
                    ssrc = SM.ap()[:, 32 + 4 * r2:32 + 4 * r2 + R]
                    lsc, rcc = LSS[idx % 4], RCS[idx % 4]
                    OP("dve", lambda v, sr=ssrc, c=lsc: v.reduce_sum(
                        sm(c), sr, axis=AX.X),
                       reads=[ssrc], writes=[sm(lsc)], label=f"ls{hh}_{j}")
                    OP("dve", lambda v, c1=rcc, c0=lsc: v.reciprocal(sm(c1), sm(c0)),
                       reads=[sm(lsc)], writes=[sm(rcc)], label=f"rc{hh}_{j}")

                def ptav_tr(idx, j):
                    R = RJ[j]
                    base = 0 if idx % 2 == 0 else 4
                    P = PSA.ap() if base == 0 else PSB.ap()
                    pds = []
                    for rr in range(R):
                        pb = base + rr

                        def trp(t, pb=pb, rr=rr, P=P):
                            last = None
                            for sub in range(4):
                                last = t.transpose(
                                    bank16(pb)[:, sub * 128:(sub + 1) * 128],
                                    P[:, (rr * 4 + sub) * 128:(rr * 4 + sub + 1) * 128],
                                    IDT.ap())
                            return last
                        OP("pe", trp, reads=[P[:, rr * 512:(rr + 1) * 512], IDT.ap()],
                           writes=[bank16(pb)], label=f"pt{hh}_{j}_{rr}")
                        pdst = PTS4[_seq[0] % 4]
                        if _seq[0] % 2 == 0:
                            OP("dve", lambda v, pb=pb, d=pdst: v.tensor_copy(
                                d, bank16(pb)[:, 0:512]),
                               reads=[bank16(pb)], writes=[pdst],
                               label=f"pv{hh}_{j}_{rr}")
                        else:
                            OP("act", lambda a, pb=pb, d=pdst: a.activation(
                                d, bank16(pb)[:, 0:512], AF.Identity),
                               reads=[bank16(pb)], writes=[pdst],
                               label=f"pv{hh}_{j}_{rr}")
                        _seq[0] += 1
                        pds.append(pdst)
                    return pds

                def ptav_av(idx, j, pds):
                    R = RJ[j]
                    r2 = idx % 2
                    avb = psf[:, 7 * 512 + r2 * 128:7 * 512 + (r2 + 1) * 128]
                    for rr in range(R):
                        pdst = pds[rr]

                        def avm(t, rr=rr, R=R, pdst=pdst, hp=hp, avb=avb,
                                vt=VT):
                            last = None
                            for sub in range(4):
                                kk = rr * 4 + sub
                                last = t.matmul(
                                    avb,
                                    pdst[:, sub * 128:(sub + 1) * 128],
                                    vt[:, kk * 256 + hp * 128:kk * 256 + (hp + 1) * 128],
                                    start=(kk == 0), stop=(kk == R * 4 - 1))
                            return last
                        OP("pe", avm, reads=[pdst, VT], writes=[avb],
                           label=f"av{hh}_{j}_{rr}")

                _seq = [0]

                def evac_ae(idx, j):
                    r2 = idx % 2
                    avb = psf[:, 7 * 512 + r2 * 128:7 * 512 + (r2 + 1) * 128]
                    tmp = H1T.ap()[:, r2 * 1024:r2 * 1024 + 128]
                    rcc = RCS[idx % 4]
                    OP("act", lambda a, t=tmp, av=avb, c=rcc: a.activation(
                        t, av, AF.Identity, scale=sm(c)),
                       reads=[avb, sm(rcc)], writes=[tmp], label=f"ae{hh}_{j}")

                def evac_at(idx, j):
                    r2 = idx % 2
                    tmp = H1T.ap()[:, r2 * 1024:r2 * 1024 + 128]
                    ats = psb[:, 7 * 1024 + 512 + r2 * 128:7 * 1024 + 512 + (r2 + 1) * 128]
                    OP("pe", lambda t, tm=tmp, d=ats: t.transpose(d, tm, IDT.ap()),
                       reads=[tmp, IDT.ap()], writes=[ats], label=f"at{hh}_{j}")
                    adst = a2b[:, hh * 1024 + j * 128:hh * 1024 + (j + 1) * 128]
                    OP("act", lambda a, d=adst, s=ats, hh=hh: a.activation(
                        d, s, AF.Identity, bias=BVP.ap()[:, hh:hh + 1]),
                       reads=[ats, BVP.ap()], writes=[adst], label=f"af{hh}_{j}")

                prev = None
                prev2 = None
                for idx, j in enumerate(QORD):
                    s_group(idx, j)
                    softmax_max(idx, j)
                    if prev2 is not None:
                        evac_ae(*prev2)
                    pds = ptav_tr(*prev) if prev is not None else None
                    softmax_exp(idx, j)
                    if prev2 is not None:
                        evac_at(*prev2)
                    if hp == 0 and idx >= 4:
                        kproj_kt(1, idx - 4)
                    if prev is not None:
                        ptav_av(*prev, pds)
                    if hp == 1 and pr < 7:
                        emit_vmm(pr + 1, 2 * idx, 6)
                        emit_vmm(pr + 1, 2 * idx + 1, 6)
                    prev2, prev = prev, (idx, j)
                evac_ae(*prev2)
                pds = ptav_tr(*prev)
                evac_at(*prev2)
                ptav_av(*prev, pds)
                evac_ae(*prev)
                evac_at(*prev)
            wload(woB[:, pr * 2048:(pr + 1) * 2048], wo_in, 8192 + pr * 2048,
                  2048, f"woB{pr}")

        attnF = a2b

        def wo_chunk(c):
            if c < 4:
                return woA[:, c * 2048:(c + 1) * 2048]
            if c < 12:
                return woB[:, (c - 4) * 2048:(c - 3) * 2048]
            if c == 12:
                return WT0.ap()
            if c == 13:
                return WT1.ap()
            return A5.ap()[:, (c - 12) * 2048:(c - 11) * 2048]

        wload(A5.ap()[:, 6144:8192], wo_in, 30720, 2048, "woC15")

        h2F = a2b
        xa = A1.ap()
        pend2 = None
        for tt in range(NO):
            xload(xo_in.ap()[tt * 128:(tt + 1) * 128, :], tt % 2, queue="sp")
            for dm in range(4):
                bko = dm % 2

                def omm1(t, b=bko, tt=tt, dm=dm):
                    last = None
                    for c in range(14):
                        last = t.matmul(
                            bank(b),
                            attnF[:, c * 1024 + tt * 128:c * 1024 + (tt + 1) * 128],
                            wo_chunk(c)[:, dm * 512:(dm + 1) * 512],
                            start=(c == 0), stop=False)
                    return last
                rdl = [attnF[:, c * 1024 + tt * 128:c * 1024 + (tt + 1) * 128]
                       for c in range(14)]
                rdl += [wo_chunk(c) for c in range(14)]
                OP("pe", omm1, reads=rdl, writes=[bank(bko)], label=f"o{tt}_{dm}a")

                def omm2(t, b=bko, tt=tt, dm=dm):
                    last = None
                    for c in range(14, NCH):
                        last = t.matmul(
                            bank(b),
                            attnF[:, c * 1024 + tt * 128:c * 1024 + (tt + 1) * 128],
                            wo_chunk(c)[:, dm * 512:(dm + 1) * 512],
                            start=False, stop=False)
                    last = t.matmul(bank(b), ONE.ap(),
                                    BOR.ap()[:, dm * 512:(dm + 1) * 512],
                                    start=False, stop=True)
                    return last
                rdl = [attnF[:, c * 1024 + tt * 128:c * 1024 + (tt + 1) * 128]
                       for c in range(14, NCH)]
                rdl += [ONE.ap(), BOR.ap()]
                rdl += [wo_chunk(c) for c in range(14, NCH)]
                OP("pe", omm2, reads=rdl, writes=[bank(bko)], label=f"o{tt}_{dm}b")
                xa_dst = xa[:, tt * 2048 + dm * 512:tt * 2048 + (dm + 1) * 512]
                xo_src = xrot[tt % 2][:, dm * 512:(dm + 1) * 512]
                OP("dve", lambda v, d=xa_dst, xs=xo_src, b=bko: v.tensor_add(
                    d, xs, bank(b)),
                   reads=[xo_src, bank(bko)], writes=[xa_dst], label=f"oe{tt}_{dm}")
            xv = xa[:, tt * 2048:(tt + 1) * 2048]
            r = ln_stats(xv, f"l2_{tt}")
            ht = ln_norm(xv, r, f"l2_{tt}")
            if pend2 is not None:
                ln_tail(*pend2)
            pend2 = (ht, h2F, 1024, tt * 128, f"l2_{tt}")
        ln_tail(*pend2)

        B0 = A6.ap()[:, 0:4096]
        B1 = A6.ap()[:, 4096:8192]
        g_hold = [A3.ap()[:, 5632 + i * 512:5632 + (i + 1) * 512] for i in range(4)]
        aF = a3b[:, 0:11264]

        def wchunk(bi, c):
            if bi == 0:
                return B0[:, c * 256:(c + 1) * 256]
            if bi == 1:
                return B1[:, c * 256:(c + 1) * 256]
            if c < 8:
                return PSA.ap()[:, c * 256:(c + 1) * 256]
            if c < 12:
                return PSB.ap()[:, (c - 8) * 256:(c - 7) * 256]
            return PTS.ap()[:, (c - 12) * 256:(c - 11) * 256]

        def wparts(bi):
            if bi == 0:
                return [(B0, 0, 4096)]
            if bi == 1:
                return [(B1, 0, 4096)]
            return [(PSA.ap(), 0, 2048), (PSB.ap(), 2048, 1024),
                    (PTS.ap(), 3072, 1024)]

        def wreads(bi):
            return [p for p, _, _ in wparts(bi)]

        slice_items = []
        grp_slices = []
        for grp in range(4):
            tiles = list(range(grp * 11, grp * 11 + 11))
            done = set()
            lst = []
            for hh2 in tiles:
                s = hh2 // 2
                if s in done:
                    continue
                done.add(s)
                lst.append((s, [t for t in (2 * s, 2 * s + 1) if t in tiles]))
                slice_items.append((wg_in, s, f"wgL{grp}_"))
                slice_items.append((wu_in, s, f"wuL{grp}_"))
            grp_slices.append(lst)

        def emit_load(i):
            if i >= len(slice_items):
                return
            w_in, s, tag = slice_items[i]
            for part, off, wdt in wparts(i % 3):
                wload(part, w_in, s * 4096 + off, wdt, f"{tag}{s}_{off}")
                OP("dve", lambda v, b=part: v.tensor_scalar_add(b, b, -128.0),
                   reads=[part], writes=[part], label=f"{tag}c{s}_{off}")

        emit_load(0)
        emit_load(1)

        def wd_tile(k):
            if k < 4:
                return A5.ap()[:, k * 2048:(k + 1) * 2048]
            if k < 8:
                return a7b[:, (k - 4) * 2048:(k - 3) * 2048]
            if k == 8:
                return WT0.ap()
            if k == 9:
                return WT1.ap()
            return H1T.ap()

        _it = [0]
        for grp in range(4):
            _wdk = [0]

            def emit_wd2(n, grp, st):
                for _ in range(n):
                    if st[0] >= 11:
                        return
                    k = st[0]
                    st[0] += 1
                    hidc = grp * 11 + k
                    dstw = wd_tile(k)
                    wload(dstw, wd_in, hidc * 2048, 2048, f"wdL{grp}_{k}")
                    OP("dve", lambda v, d=dstw: v.tensor_scalar_add(d, d, -128.0),
                       reads=[dstw], writes=[dstw], label=f"wdc{grp}_{k}")
                    OP("dve", lambda v, d=dstw: v.tensor_mul(d, d, sdbap),
                       reads=[dstw, sdbap], writes=[dstw], label=f"wds{grp}_{k}")

            for s, stiles in grp_slices[grp]:
                ig = _it[0]
                _it[0] += 1
                emit_load(ig + 2)
                gi = 0
                for t_ in stiles:
                    col = (t_ % 2) * 128
                    for tthalf in range(2):
                        bkg = gi % 2

                        def gmm(t, b=bkg, col=col, th=tthalf, bi=ig % 3):
                            last = None
                            for c in range(NCH):
                                last = t.matmul(
                                    bank(b),
                                    wchunk(bi, c)[:, col:col + 128],
                                    h2F[:, c * 1024 + th * 512:c * 1024 + (th + 1) * 512],
                                    start=(c == 0), stop=(c == NCH - 1))
                            return last
                        grd = wreads(ig % 3) + [
                            h2F[:, c * 1024 + tthalf * 512:c * 1024 + (tthalf + 1) * 512]
                            for c in range(NCH)]
                        OP("pe", gmm, reads=grd,
                           writes=[bank(bkg)], label=f"g{t_}_{tthalf}")
                        OP("act", lambda a, b=bkg, d=g_hold[gi % 4], hh2=t_: a.activation(
                            d, bank(b), AF.Silu, scale=SGP.ap()[:, hh2:hh2 + 1]),
                           reads=[bank(bkg), SGP.ap()], writes=[g_hold[gi % 4]],
                           label=f"ge{t_}_{tthalf}")
                        gi += 1
                emit_wd2(2, grp, _wdk)
                iu = _it[0]
                _it[0] += 1
                emit_load(iu + 2)
                gi = 0
                for t_ in stiles:
                    col = (t_ % 2) * 128
                    k_ = t_ - grp * 11
                    for tthalf in range(2):
                        bku = 2 + gi % 2

                        def ummf(t, b=bku, col=col, th=tthalf, bi=iu % 3):
                            last = None
                            for c in range(NCH):
                                last = t.matmul(
                                    bank(b),
                                    wchunk(bi, c)[:, col:col + 128],
                                    h2F[:, c * 1024 + th * 512:c * 1024 + (th + 1) * 512],
                                    start=(c == 0), stop=(c == NCH - 1))
                            return last
                        urd = wreads(iu % 3) + [
                            h2F[:, c * 1024 + tthalf * 512:c * 1024 + (tthalf + 1) * 512]
                            for c in range(NCH)]
                        OP("pe", ummf, reads=urd,
                           writes=[bank(bku)], label=f"u{t_}_{tthalf}")
                        adst = aF[:, k_ * 1024 + tthalf * 512:k_ * 1024 + (tthalf + 1) * 512]
                        OP("act", lambda a, b=bku, d=adst, hh2=t_: a.activation(
                            d, bank(b), AF.Identity, scale=SUP.ap()[:, hh2:hh2 + 1]),
                           reads=[bank(bku), SUP.ap()], writes=[adst],
                           label=f"ue{t_}_{tthalf}")
                        OP("dve", lambda v, d=adst, g=g_hold[gi % 4]: v.tensor_mul(
                            d, g, d),
                           reads=[g_hold[gi % 4], adst], writes=[adst],
                           label=f"a{t_}_{tthalf}")
                        gi += 1
            emit_wd2(11, grp, _wdk)
            for tt in range(NO):
                for dm in range(4):
                    bkd = 4 + (dm % 2)

                    def dmm(t, b=bkd, tt=tt, dm=dm):
                        last = None
                        for k in range(11):
                            last = t.matmul(
                                bank(b),
                                aF[:, k * 1024 + tt * 128:k * 1024 + (tt + 1) * 128],
                                wd_tile(k)[:, dm * 512:(dm + 1) * 512],
                                start=(k == 0), stop=(k == 10))
                        return last
                    rdl = [aF] + [wd_tile(k) for k in range(11)]
                    OP("pe", dmm, reads=rdl, writes=[bank(bkd)],
                       label=f"d{grp}_{tt}_{dm}")
                    xv = xa[:, tt * 2048 + dm * 512:tt * 2048 + (dm + 1) * 512]
                    OP("dve", lambda v, x=xv, b=bkd: v.tensor_tensor(
                        x, x, bank(b), ALU.add),
                       reads=[xv, bank(bkd)], writes=[xv], label=f"de{grp}_{tt}_{dm}")

        for tt in range(NO):
            src = xa[:, tt * 2048:(tt + 1) * 2048]
            OP("sp", lambda s, sr=src, tt=tt: s.dma_start(
                out=y_out.ap()[tt * 128:(tt + 1) * 128, :], in_=sr),
               reads=[src], writes=[], label=f"st{tt}")
        for i in range(8):
            se = sch.sems[f"dmas{i}"]
            if se.n:
                sch.raw("sp", lambda s, h=se.h, v=se.n: s.wait_ge(h, v))

        block = E(nc.Block())
        sch.emit(block)

    return nc



def _mask_host(parity):
    d0 = parity * 128
    jj = np.arange(768)[None, :]
    row = np.arange(128)[:, None]
    m = np.where(jj <= row + d0 + 256, 0.0, NEG).astype(np.float32)
    return m.astype(ml_dtypes.bfloat16)


def _shared_weights(inp):
    wq = np.asarray(inp["L0_attn_Wq"])
    wk = np.asarray(inp["L0_attn_Wk"])
    wv = np.asarray(inp["L0_attn_Wv"])
    wo = np.asarray(inp["L0_attn_Wo"])
    wg = np.asarray(inp["L0_ffn_wq_gate"])
    wu = np.asarray(inp["L0_ffn_wq_up"])
    wd = np.asarray(inp["L0_ffn_wq_down"])

    def qkv_pp(w):
        return np.ascontiguousarray(
            w.reshape(8, 256, 16, 128).transpose(3, 0, 2, 1).reshape(128, 32768))

    def gu_pp(w):
        return np.ascontiguousarray(
            w.reshape(22, 256, 16, 128).transpose(3, 0, 2, 1).reshape(128, 90112))

    return {
        "wqp": qkv_pp(wq),
        "wkp": qkv_pp(wk),
        "wvp": qkv_pp(wv),
        "wop": np.ascontiguousarray(
            wo.reshape(2048, 16, 128).transpose(2, 1, 0).reshape(128, 32768)),
        "wgp": gu_pp(wg),
        "wup": gu_pp(wu),
        "wdp": np.ascontiguousarray(
            wd.reshape(2048, 44, 128).transpose(2, 1, 0).reshape(128, 90112)),
        "bq_pp": np.ascontiguousarray(np.asarray(inp["L0_attn_bq"]).reshape(16, 128).T),
        "bk_pp": np.ascontiguousarray(np.asarray(inp["L0_attn_bk"]).reshape(16, 128).T),
        "bv_pp": np.ascontiguousarray(np.asarray(inp["L0_attn_bv"]).reshape(16, 128).T),
        "bo_row": np.ascontiguousarray(np.asarray(inp["L0_attn_bo"]).reshape(1, 2048)),
        "sg_pp": np.ascontiguousarray(np.asarray(inp["L0_ffn_s_gate"]).reshape(44, 128).T),
        "su_pp": np.ascontiguousarray(np.asarray(inp["L0_ffn_s_up"]).reshape(44, 128).T),
        "sdb": np.ascontiguousarray(
            np.broadcast_to(np.asarray(inp["L0_ffn_s_down"]), (128, 2048))),
        "ident": np.eye(128, dtype=np.float32).astype(ml_dtypes.bfloat16),
    }


def _core_inputs(c, inp, shared):
    b, p = c // 2, c % 2
    X = np.asarray(inp["L0_attn_X"])
    rows = np.concatenate([np.arange((p + 2 * j) * 128, (p + 2 * j) * 128 + 128)
                           for j in range(8)])
    d = dict(shared)
    d["x"] = np.ascontiguousarray(X[b])
    d["xown"] = np.ascontiguousarray(X[b][rows])
    d["mask"] = _mask_host(p)
    return d, b, rows


def kernel(**inputs):
    nc = build_nc()
    shared = _shared_weights(inputs)
    in_maps = []
    meta = []
    for c in range(8):
        m, b, rows = _core_inputs(c, inputs, shared)
        in_maps.append(m)
        meta.append((b, rows))
    res = run_bass_kernel_spmd(nc, in_maps, core_ids=list(range(8)))
    out = np.empty((4, 2048, 2048), dtype=np.float32)
    for c in range(8):
        b, rows = meta[c]
        out[b][rows] = np.asarray(res.results[c]["y"], dtype=np.float32)
    return out


# version 22
# speedup vs baseline: 1.0707x; 1.0707x over previous
import numpy as np
import ml_dtypes
from contextlib import ExitStack
import concourse.bass as bass
import concourse.mybir as mybir
from concourse.bass_utils import run_bass_kernel_spmd

F32 = mybir.dt.float32
BF16 = mybir.dt.bfloat16
I32 = mybir.dt.int32
AF = mybir.ActivationFunctionType
ALU = mybir.AluOpType
AX = mybir.AxisListType

D = 2048
SEQ = 2048
H = 16
NCH = 16
NT = 16
NO = 8
NHT = 44
HID = 5632
SCL = float(128.0 ** -0.25)
EPS = 1e-5
RJ = [1, 1, 2, 2, 3, 3, 4, 4]
QORD = [7, 0, 6, 1, 5, 2, 4, 3]
LSS = [0, 1, 42, 43]
RCS = [2, 3, 40, 41]
NEG = -1e30

_SB_GRAN = 256
_PS_GRAN = 2048
_DR_GRAN = 1 << 18


class _Sem:
    __slots__ = ("name", "n", "snaps", "h")

    def __init__(self, name):
        self.name = name
        self.n = 0
        self.snaps = {}
        self.h = None


class _Sched:
    def __init__(self):
        self.streams = {k: [] for k in ("pe", "dve", "act", "pool", "sp")}
        self.know = {k: {} for k in self.streams}
        self.slots = {}
        self.sems = {k: _Sem(k) for k in ("pe", "dve", "act")}
        for i in range(8):
            self.sems[f"dmag{i}"] = _Sem(f"dmag{i}")
            self.sems[f"dmas{i}"] = _Sem(f"dmas{i}")
        self.rot = {"pool": 0, "sp": 0}

    def _keys(self, ap):
        t = ap.tensor
        tn = type(t).__name__
        esz = mybir.dt.size(ap.dtype)
        dims = [list(d) for d in ap.ap]
        name = t.name
        if tn == "DRamTensorHandle":
            span = 1 + sum(s * (c - 1) for s, c in dims)
            lo, hi = ap.offset * esz, (ap.offset + span) * esz
            gran, psum = _DR_GRAN, False
        else:
            psum = tn == "PSumTensorHandle"
            pstride = dims[0][0] if dims else 1
            off = ap.offset % pstride if pstride > 0 else ap.offset
            free = dims[1:]
            span = 1 + sum(s * (c - 1) for s, c in free) if free else 1
            lo, hi = off * esz, (off + span) * esz
            gran = 4 if name == "SM" else (_PS_GRAN if psum else _SB_GRAN)
        return {(name, b) for b in range(lo // gran, (hi - 1) // gran + 1)}, psum

    def op(self, eng, build, reads=(), writes=(), label=""):
        if eng in ("pool", "sp"):
            pfx = "dmag" if eng == "pool" else "dmas"
            sem = self.sems[f"{pfx}{self.rot[eng] % 8}"]
            self.rot[eng] += 1
            amt = 16
        else:
            sem = self.sems[eng]
            amt = 1
        waits = {}
        if amt == 16 and sem.n > 0:
            waits[sem] = sem.n

        def addw(sv):
            if sv is None:
                return
            s, v = sv
            if eng == "pe" and s is self.sems["pe"]:
                return
            if waits.get(s, -1) < v:
                waits[s] = v

        rd, wr = set(), set()
        for ap in reads:
            k, psum = self._keys(ap)
            (wr if psum else rd).update(k)
        for ap in writes:
            k, _ = self._keys(ap)
            wr.update(k)
        rd -= wr
        for k in rd:
            sl = self.slots.get(k)
            if sl:
                addw(sl["w"])
        for k in wr:
            sl = self.slots.get(k)
            if sl:
                addw(sl["w"])
                for r in sl["rs"]:
                    addw(r)
        kn = self.know[eng]
        real = [(s, v) for s, v in waits.items() if kn.get(s, 0) < v]
        sem.n += amt
        val = sem.n
        for s, v in real:
            if kn.get(s, 0) < v:
                kn[s] = v
            snap = s.snaps.get(v)
            if snap:
                for s2, v2 in snap.items():
                    if kn.get(s2, 0) < v2:
                        kn[s2] = v2
        pub = dict(kn)
        pub[sem] = val
        sem.snaps[val] = pub
        for k in wr:
            self.slots[k] = {"w": (sem, val), "rs": []}
        for k in rd:
            sl = self.slots.setdefault(k, {"w": None, "rs": []})
            sl["rs"].append((sem, val))
        self.streams[eng].append((real, build, sem, amt, label))

    def raw(self, eng, build):
        self.streams[eng].append(([], build, None, 0, "raw"))

    def emit(self, block):
        deco = {"pe": block.tensor, "dve": block.vector, "act": block.scalar,
                "pool": block.gpsimd, "sp": block.sync}

        def runner(stream):
            def run(eng):
                for waits, build, sem, amt, label in stream:
                    for s, v in waits:
                        eng.wait_ge(s.h, v)
                    inst = build(eng)
                    if sem is not None:
                        inst.then_inc(sem.h, amt)
            return run

        for name in self.streams:
            deco[name](runner(self.streams[name]))


def build_nc():
    nc = bass.Bass("TRN2", target_bir_lowering=False)

    x_in = nc.dram_tensor("x", [SEQ, D], F32, kind="ExternalInput")
    xo_in = nc.dram_tensor("xown", [1024, D], F32, kind="ExternalInput")
    wq_in = nc.dram_tensor("wqp", [128, 8 * 4096], F32, kind="ExternalInput")
    wk_in = nc.dram_tensor("wkp", [128, 8 * 4096], F32, kind="ExternalInput")
    wv_in = nc.dram_tensor("wvp", [128, 8 * 4096], F32, kind="ExternalInput")
    wo_in = nc.dram_tensor("wop", [128, 16 * 2048], F32, kind="ExternalInput")
    bq_in = nc.dram_tensor("bq_pp", [128, H], F32, kind="ExternalInput")
    bk_in = nc.dram_tensor("bk_pp", [128, H], F32, kind="ExternalInput")
    bv_in = nc.dram_tensor("bv_pp", [128, H], F32, kind="ExternalInput")
    bo_in = nc.dram_tensor("bo_row", [1, D], F32, kind="ExternalInput")
    wg_in = nc.dram_tensor("wgp", [128, 22 * 4096], I32, kind="ExternalInput")
    wu_in = nc.dram_tensor("wup", [128, 22 * 4096], I32, kind="ExternalInput")
    wd_in = nc.dram_tensor("wdp", [128, 44 * 2048], I32, kind="ExternalInput")
    sg_in = nc.dram_tensor("sg_pp", [128, NHT], F32, kind="ExternalInput")
    su_in = nc.dram_tensor("su_pp", [128, NHT], F32, kind="ExternalInput")
    sdb_in = nc.dram_tensor("sdb", [128, D], F32, kind="ExternalInput")
    mask_in = nc.dram_tensor("mask", [128, 768], BF16, kind="ExternalInput")
    id_in = nc.dram_tensor("ident", [128, 128], BF16, kind="ExternalInput")
    y_out = nc.dram_tensor("y", [1024, D], F32, kind="ExternalOutput")

    with ExitStack() as ctx:
        E = ctx.enter_context
        A1 = E(nc.sbuf_tensor("A1", [128, 16384], F32))
        A2 = E(nc.sbuf_tensor("A2", [128, 8192], F32))
        A3 = E(nc.sbuf_tensor("A3", [128, 8192], F32))
        A5 = E(nc.sbuf_tensor("A5", [128, 8192], BF16))
        A6 = E(nc.sbuf_tensor("A6", [128, 8192], BF16))
        A7 = E(nc.sbuf_tensor("A7", [128, 4096], F32))
        WT0 = E(nc.sbuf_tensor("WT0", [128, 2048], BF16))
        WT1 = E(nc.sbuf_tensor("WT1", [128, 2048], BF16))
        H1T = E(nc.sbuf_tensor("H1T", [128, 2048], BF16))
        PSA = E(nc.sbuf_tensor("PSA", [128, 2048], BF16))
        PSB = E(nc.sbuf_tensor("PSB", [128, 1024], BF16))
        PTS = E(nc.sbuf_tensor("PTS", [128, 1024], BF16))
        MSK = E(nc.sbuf_tensor("MSK", [128, 768], BF16))
        IDT = E(nc.sbuf_tensor("IDT", [128, 128], BF16))
        BOR = E(nc.sbuf_tensor("BOR", [1, D], BF16))
        ONE = E(nc.sbuf_tensor("ONE", [1, 128], BF16))
        SGP = E(nc.sbuf_tensor("SGP", [128, NHT], F32))
        SUP = E(nc.sbuf_tensor("SUP", [128, NHT], F32))
        SDB = E(nc.sbuf_tensor("SDB", [128, D], BF16))
        SM = E(nc.sbuf_tensor("SM", [128, 48], F32))
        BNS = E(nc.sbuf_tensor("BNS", [128, 48], F32))
        BQS = E(nc.sbuf_tensor("BQS", [128, H], F32))
        BKS = E(nc.sbuf_tensor("BKS", [128, H], F32))
        BVP = E(nc.sbuf_tensor("BVP", [128, H], F32))
        PS = E(nc.psum_tensor("PS", [128, 4096], F32))

        sch = _Sched()
        for k, s in sch.sems.items():
            s.h = E(nc.semaphore(k))

        a1b = A1.ap().bitcast(BF16)
        a2b = A2.ap().bitcast(BF16)
        a3b = A3.ap().bitcast(BF16)
        a7b = A7.ap().bitcast(BF16)
        psf = PS.ap()
        psb = PS.ap().bitcast(BF16)
        sdbap = SDB.ap()

        def bank(i, n=1):
            return psf[:, i * 512:(i + n) * 512]

        def bankw(i, w):
            return psf[:, i * 512:i * 512 + w]

        def bank16(i):
            return psb[:, i * 1024:(i + 1) * 1024]

        def sm(c):
            return SM.ap()[:, c:c + 1]

        OP = sch.op

        def ld(dst, src, label=""):
            OP("pool", lambda g, d=dst, s=src: g.dma_start(out=d, in_=s),
               reads=[src], writes=[dst], label=label)

        ld(IDT.ap(), id_in.ap(), "ident")
        ld(MSK.ap(), mask_in.ap(), "mask")
        ld(BOR.ap(), bo_in.ap(), "bo")
        ld(SGP.ap(), sg_in.ap(), "sg")
        ld(SUP.ap(), su_in.ap(), "su")
        ld(SDB.ap(), sdb_in.ap(), "sdb")
        ld(BQS.ap(), bq_in.ap(), "bq")
        ld(BKS.ap(), bk_in.ap(), "bk")
        ld(BVP.ap(), bv_in.ap(), "bv")
        OP("dve", lambda v: v.memset(ONE.ap(), 1.0), writes=[ONE.ap()], label="ones")
        OP("dve", lambda v: v.memset(sm(22), EPS), writes=[sm(22)], label="eps")
        OP("dve", lambda v: v.tensor_scalar_mul(BQS.ap(), BQS.ap(), SCL),
           reads=[BQS.ap()], writes=[BQS.ap()], label="bqs")
        OP("dve", lambda v: v.tensor_scalar_mul(BKS.ap(), BKS.ap(), SCL),
           reads=[BKS.ap()], writes=[BKS.ap()], label="bks")

        xrot = [A7.ap()[:, 0:2048], A7.ap()[:, 2048:4096]]

        def wload(dst, w_in, c0, w, label=""):
            src = w_in.ap()[:, c0:c0 + w]
            OP("pool", lambda g, d=dst, s=src: g.dma_start(out=d, in_=s),
               reads=[src], writes=[dst], label=label)

        def xload(src, r, queue="sp"):
            dst = xrot[r]
            OP(queue, lambda g, d=dst, s=src: g.dma_start(out=d, in_=s),
               reads=[src], writes=[dst], label="xld")

        _ln = [0]

        def ln_stats(xap, label):
            r = _ln[0] % 2
            _ln[0] += 1
            bns = BNS.ap()[:, r * 24:(r + 1) * 24]
            for ch in range(4):
                OP("dve", lambda v, o=bns[:, ch * 6:(ch + 1) * 6],
                   x=xap[:, ch * 512:(ch + 1) * 512]: v.bn_stats(o, x),
                   reads=[xap[:, ch * 512:(ch + 1) * 512]],
                   writes=[bns[:, ch * 6:(ch + 1) * 6]], label=label + f"s{ch}")
            c_mv, c_std, c_rstd = 8 + 2 * r, 16 + r, 18 + r
            mv = SM.ap()[:, c_mv:c_mv + 2]
            OP("dve", lambda v, o=mv, s=bns: v.bn_aggr(o, s),
               reads=[bns], writes=[mv], label=label + "a")
            OP("act", lambda a: a.activation(sm(c_std), sm(c_mv + 1), AF.Sqrt,
                                             bias=sm(22)),
               reads=[sm(c_mv + 1), sm(22)], writes=[sm(c_std)], label=label + "d")
            OP("dve", lambda v: v.reciprocal(sm(c_rstd), sm(c_std)),
               reads=[sm(c_std)], writes=[sm(c_rstd)], label=label + "r")
            return r

        def ln_norm(xap, r, label):
            c_mean, c_rstd = 8 + 2 * r, 18 + r
            htile = H1T.ap() if r == 0 else PSA.ap()
            OP("dve", lambda v, x=xap, o=htile: v.tensor_scalar(
                o, x, sm(c_mean), sm(c_rstd), ALU.subtract, ALU.mult),
               reads=[xap, sm(c_mean), sm(c_rstd)], writes=[htile], label=label + "n")
            return htile

        def ln_tail(htile, outFb, cstride, tcol, label):
            for half in range(2):
                bkx = 6 + half

                def tr(t, h=htile, b=bkx, hf=half):
                    last = None
                    for c in range(8):
                        last = t.transpose(
                            bank16(b)[:, c * 128:(c + 1) * 128],
                            h[:, (hf * 8 + c) * 128:(hf * 8 + c + 1) * 128],
                            IDT.ap())
                    return last

                OP("pe", tr, reads=[htile, IDT.ap()], writes=[bank16(bkx)],
                   label=label + f"T{half}")
                seg = outFb[:, half * 8 * cstride:(half * 8 + 8) * cstride]
                dst = seg.rearrange("p (a b) -> p a b", a=8, b=cstride)[:, :, tcol:tcol + 128]
                OP("act", lambda a, b=bkx, d=dst: a.activation(
                    d, bank16(b)[:, 0:1024], AF.Identity),
                   reads=[bank16(bkx)], writes=[dst], label=label + f"E{half}")

        h1F = a1b
        h1oF = a2b
        QF = a3b

        a5f = A5.ap().bitcast(F32)
        xrot4 = [A7.ap()[:, 0:2048], A7.ap()[:, 2048:4096],
                 a5f[:, 0:2048], a5f[:, 2048:4096]]

        def xload4(src, r):
            dst = xrot4[r]
            OP("sp", lambda g, d=dst, s=src: g.dma_start(out=d, in_=s),
               reads=[src], writes=[dst], label="xld")

        wqbuf = [A6.ap()[:, 0:4096], A6.ap()[:, 4096:8192]]

        def emit_q(pr):
            buf = wqbuf[pr % 2]
            wload(buf, wq_in, pr * 4096, 4096, f"wqL{pr}")
            for hp in range(2):
                hh = pr * 2 + hp
                for half in range(2):
                    bkq = half

                    def qmm(t, b=bkq, bu=buf, hp=hp, hf=half):
                        last = None
                        for c in range(NCH):
                            last = t.matmul(
                                bank(b),
                                bu[:, c * 256 + hp * 128:c * 256 + (hp + 1) * 128],
                                h1oF[:, c * 1024 + hf * 512:c * 1024 + (hf + 1) * 512],
                                start=(c == 0), stop=(c == NCH - 1))
                        return last
                    qrd = [buf] + [h1oF[:, c * 1024 + half * 512:c * 1024 + (half + 1) * 512]
                                   for c in range(NCH)]
                    OP("pe", qmm, reads=qrd, writes=[bank(bkq)],
                       label=f"q{hh}_{half}")
                    dst = QF[:, hh * 1024 + half * 512:hh * 1024 + (half + 1) * 512]
                    OP("act", lambda a, b=bkq, d=dst, hh=hh: a.activation(
                        d, bank(b), AF.Identity, bias=BQS.ap()[:, hh:hh + 1],
                        scale=SCL),
                       reads=[bank(bkq), BQS.ap()], writes=[dst], label=f"qe{hh}")

        ln_jobs = [(xo_in.ap()[i * 128:(i + 1) * 128, :], h1oF, 1024, i * 128, f"lo_{i}")
                   for i in range(NO)]
        ln_jobs += [(x_in.ap()[i * 128:(i + 1) * 128, :], h1F, 2048, i * 128, f"l1_{i}")
                    for i in range(NT)]
        xload4(ln_jobs[0][0], 0)
        xload4(ln_jobs[1][0], 1)
        pend = None
        for i, (src, outFb, cs, tc, lab) in enumerate(ln_jobs):
            r = ln_stats(xrot4[i % 4], lab)
            ht = ln_norm(xrot4[i % 4], r, lab)
            if i + 2 < len(ln_jobs):
                xload4(ln_jobs[i + 2][0], (i + 2) % 4)
            if pend is not None:
                ln_tail(*pend)
            pend = (ht, outFb, cs, tc, lab)
            if 9 <= i <= 16:
                emit_q(i - 9)
        ln_tail(*pend)

        wvbuf = A6.ap()[:, 0:4096]
        wkbuf = A6.ap()[:, 4096:8192]
        VT = A5.ap()[:, 0:4096]
        kbuf = [A5.ap()[:, 4096:6144], A5.ap()[:, 6144:8192]]
        PTS4 = [PTS.ap()[:, 0:512], PTS.ap()[:, 512:1024],
                H1T.ap()[:, 512:1024], H1T.ap()[:, 1536:2048]]
        woA = A6.ap()
        woB = a3b
        wload(WT0.ap(), wo_in, 24576, 2048, "woC12")
        wload(WT1.ap(), wo_in, 26624, 2048, "woC13")

        for pr in range(8):
            wload(wvbuf, wv_in, pr * 4096, 4096, f"wvL{pr}")
            for kt in range(NT):
                bkv = kt % 4

                def vmm(t, b=bkv, kt=kt):
                    last = None
                    for c in range(NCH):
                        last = t.matmul(
                            bankw(b, 256),
                            h1F[:, c * 2048 + kt * 128:c * 2048 + (kt + 1) * 128],
                            wvbuf[:, c * 256:(c + 1) * 256],
                            start=(c == 0), stop=(c == NCH - 1))
                    return last
                vrd = [wvbuf] + [h1F[:, c * 2048 + kt * 128:c * 2048 + (kt + 1) * 128]
                                 for c in range(NCH)]
                OP("pe", vmm, reads=vrd, writes=[bank(bkv)],
                   label=f"v{pr}_{kt}")
                dst = VT[:, kt * 256:(kt + 1) * 256]
                OP("act", lambda a, b=bkv, d=dst: a.activation(d, bankw(b, 256),
                                                               AF.Identity),
                   reads=[bank(bkv)], writes=[dst], label=f"ve{pr}_{kt}")
            if pr == 0:
                wload(wkbuf, wk_in, 0, 4096, "wkL0")
            if pr == 7:
                wload(woA[:, 0:4096], wo_in, 0, 4096, "woA0")

            def kproj_kt(hp2, kt, prv=pr):
                hh2 = prv * 2 + hp2
                kb2 = kbuf[hp2]

                def kmm(t, kt=kt, hp2=hp2):
                    last = None
                    for c in range(NCH):
                        last = t.matmul(
                            bank(6),
                            wkbuf[:, c * 256 + hp2 * 128:c * 256 + (hp2 + 1) * 128],
                            h1F[:, c * 2048 + kt * 512:c * 2048 + (kt + 1) * 512],
                            start=(c == 0), stop=(c == NCH - 1))
                    return last
                krd = [wkbuf] + [h1F[:, c * 2048 + kt * 512:c * 2048 + (kt + 1) * 512]
                                 for c in range(NCH)]
                OP("pe", kmm, reads=krd, writes=[bank(6)],
                   label=f"k{hh2}_{kt}")
                dst = kb2[:, kt * 512:(kt + 1) * 512]
                OP("act", lambda a, d=dst, hh2=hh2: a.activation(
                    d, bank(6), AF.Identity, bias=BKS.ap()[:, hh2:hh2 + 1],
                    scale=SCL),
                   reads=[bank(6), BKS.ap()], writes=[dst], label=f"ke{hh2}")

            for hp in range(2):
                hh = pr * 2 + hp
                kb = kbuf[hp]
                if hp == 0:
                    if pr == 0:
                        for kt in range(4):
                            kproj_kt(0, kt)
                elif pr == 7:
                    wload(woA[:, 4096:8192], wo_in, 4096, 4096, "woA1")
                    wload(A5.ap()[:, 4096:6144], wo_in, 28672, 2048, "woC14")
                else:
                    wload(wkbuf, wk_in, (pr + 1) * 4096, 4096, f"wkL{pr + 1}")

                def s_group(idx, j):
                    R = RJ[j]
                    base = 0 if idx % 2 == 0 else 4
                    moff = 256 if j % 2 == 0 else 0
                    r2 = idx % 2
                    for rr in range(R):
                        lastb = rr == R - 1

                        def smm(t, rr=rr, base=base, j=j, hh=hh, moff=moff,
                                kb=kb, lastb=lastb):
                            last = t.matmul(
                                bank(base + rr),
                                QF[:, hh * 1024 + j * 128:hh * 1024 + (j + 1) * 128],
                                kb[:, rr * 512:(rr + 1) * 512],
                                start=True, stop=not lastb)
                            if lastb:
                                last = t.matmul(bank(base + rr), IDT.ap(),
                                                MSK.ap()[:, moff:moff + 512],
                                                start=False, stop=True)
                            return last
                        OP("pe", smm,
                           reads=[QF[:, hh * 1024 + j * 128:hh * 1024 + (j + 1) * 128],
                                  kb[:, rr * 512:(rr + 1) * 512], IDT.ap(), MSK.ap()],
                           writes=[bank(base + rr)], label=f"s{hh}_{j}_{rr}")
                        OP("dve", lambda v, rr=rr, base=base, r2=r2: v.reduce_max(
                            sm(24 + 4 * r2 + rr), bank(base + rr), axis=AX.X),
                           reads=[bank(base + rr)],
                           writes=[sm(24 + 4 * r2 + rr)], label=f"mxp{hh}_{j}_{rr}")

                def softmax_max(idx, j):
                    R = RJ[j]
                    r2 = idx % 2
                    src = SM.ap()[:, 24 + 4 * r2:24 + 4 * r2 + R]
                    OP("dve", lambda v, sr=src, r2=r2: v.reduce_max(
                        sm(4 + r2), sr, axis=AX.X, negate=True),
                       reads=[src], writes=[sm(4 + r2)], label=f"mx{hh}_{j}")

                def softmax_exp(idx, j):
                    R = RJ[j]
                    base = 0 if idx % 2 == 0 else 4
                    r2 = idx % 2
                    P = PSA.ap() if base == 0 else PSB.ap()
                    for rr in range(R):
                        psv = psf[:, (base + rr) * 512:(base + rr + 1) * 512]
                        Pv = P[:, rr * 512:(rr + 1) * 512]
                        OP("act", lambda a, p=psv, o=Pv, r2=r2, rr=rr: a.activation(
                            o, p, AF.Exp, bias=sm(4 + r2), scale=1.0,
                            accum_out=sm(32 + 4 * r2 + rr)),
                           reads=[psv, sm(4 + r2)],
                           writes=[Pv, sm(32 + 4 * r2 + rr)], label=f"ex{hh}_{j}_{rr}")
                    ssrc = SM.ap()[:, 32 + 4 * r2:32 + 4 * r2 + R]
                    lsc, rcc = LSS[idx % 4], RCS[idx % 4]
                    OP("dve", lambda v, sr=ssrc, c=lsc: v.reduce_sum(
                        sm(c), sr, axis=AX.X),
                       reads=[ssrc], writes=[sm(lsc)], label=f"ls{hh}_{j}")
                    OP("dve", lambda v, c1=rcc, c0=lsc: v.reciprocal(sm(c1), sm(c0)),
                       reads=[sm(lsc)], writes=[sm(rcc)], label=f"rc{hh}_{j}")

                def ptav_tr(idx, j):
                    R = RJ[j]
                    base = 0 if idx % 2 == 0 else 4
                    P = PSA.ap() if base == 0 else PSB.ap()
                    pds = []
                    for rr in range(R):
                        pb = base + rr

                        def trp(t, pb=pb, rr=rr, P=P):
                            last = None
                            for sub in range(4):
                                last = t.transpose(
                                    bank16(pb)[:, sub * 128:(sub + 1) * 128],
                                    P[:, (rr * 4 + sub) * 128:(rr * 4 + sub + 1) * 128],
                                    IDT.ap())
                            return last
                        OP("pe", trp, reads=[P[:, rr * 512:(rr + 1) * 512], IDT.ap()],
                           writes=[bank16(pb)], label=f"pt{hh}_{j}_{rr}")
                        pdst = PTS4[_seq[0] % 4]
                        if _seq[0] % 2 == 0:
                            OP("dve", lambda v, pb=pb, d=pdst: v.tensor_copy(
                                d, bank16(pb)[:, 0:512]),
                               reads=[bank16(pb)], writes=[pdst],
                               label=f"pv{hh}_{j}_{rr}")
                        else:
                            OP("act", lambda a, pb=pb, d=pdst: a.activation(
                                d, bank16(pb)[:, 0:512], AF.Identity),
                               reads=[bank16(pb)], writes=[pdst],
                               label=f"pv{hh}_{j}_{rr}")
                        _seq[0] += 1
                        pds.append(pdst)
                    return pds

                def ptav_av(idx, j, pds):
                    R = RJ[j]
                    r2 = idx % 2
                    avb = psf[:, 7 * 512 + r2 * 128:7 * 512 + (r2 + 1) * 128]
                    for rr in range(R):
                        pdst = pds[rr]

                        def avm(t, rr=rr, R=R, pdst=pdst, hp=hp, avb=avb):
                            last = None
                            for sub in range(4):
                                kk = rr * 4 + sub
                                last = t.matmul(
                                    avb,
                                    pdst[:, sub * 128:(sub + 1) * 128],
                                    VT[:, kk * 256 + hp * 128:kk * 256 + (hp + 1) * 128],
                                    start=(kk == 0), stop=(kk == R * 4 - 1))
                            return last
                        OP("pe", avm, reads=[pdst, VT], writes=[avb],
                           label=f"av{hh}_{j}_{rr}")

                _seq = [0]

                def evac_ae(idx, j):
                    r2 = idx % 2
                    avb = psf[:, 7 * 512 + r2 * 128:7 * 512 + (r2 + 1) * 128]
                    tmp = H1T.ap()[:, r2 * 1024:r2 * 1024 + 128]
                    rcc = RCS[idx % 4]
                    OP("act", lambda a, t=tmp, av=avb, c=rcc: a.activation(
                        t, av, AF.Identity, scale=sm(c)),
                       reads=[avb, sm(rcc)], writes=[tmp], label=f"ae{hh}_{j}")

                def evac_at(idx, j):
                    r2 = idx % 2
                    tmp = H1T.ap()[:, r2 * 1024:r2 * 1024 + 128]
                    ats = psb[:, 7 * 1024 + 512 + r2 * 128:7 * 1024 + 512 + (r2 + 1) * 128]
                    OP("pe", lambda t, tm=tmp, d=ats: t.transpose(d, tm, IDT.ap()),
                       reads=[tmp, IDT.ap()], writes=[ats], label=f"at{hh}_{j}")
                    adst = a2b[:, hh * 1024 + j * 128:hh * 1024 + (j + 1) * 128]
                    OP("act", lambda a, d=adst, s=ats, hh=hh: a.activation(
                        d, s, AF.Identity, bias=BVP.ap()[:, hh:hh + 1]),
                       reads=[ats, BVP.ap()], writes=[adst], label=f"af{hh}_{j}")

                prev = None
                prev2 = None
                for idx, j in enumerate(QORD):
                    s_group(idx, j)
                    softmax_max(idx, j)
                    if prev2 is not None:
                        evac_ae(*prev2)
                    pds = ptav_tr(*prev) if prev is not None else None
                    softmax_exp(idx, j)
                    if prev2 is not None:
                        evac_at(*prev2)
                    if hp == 0 and idx >= 4:
                        kproj_kt(1, idx - 4)
                    if hp == 1 and pr < 7 and idx >= 4:
                        kproj_kt(0, idx - 4, prv=pr + 1)
                    if prev is not None:
                        ptav_av(*prev, pds)
                    prev2, prev = prev, (idx, j)
                evac_ae(*prev2)
                pds = ptav_tr(*prev)
                evac_at(*prev2)
                ptav_av(*prev, pds)
                evac_ae(*prev)
                evac_at(*prev)
            wload(woB[:, pr * 2048:(pr + 1) * 2048], wo_in, 8192 + pr * 2048,
                  2048, f"woB{pr}")

        attnF = a2b

        def wo_chunk(c):
            if c < 4:
                return woA[:, c * 2048:(c + 1) * 2048]
            if c < 12:
                return woB[:, (c - 4) * 2048:(c - 3) * 2048]
            if c == 12:
                return WT0.ap()
            if c == 13:
                return WT1.ap()
            return A5.ap()[:, (c - 12) * 2048:(c - 11) * 2048]

        wload(A5.ap()[:, 6144:8192], wo_in, 30720, 2048, "woC15")

        h2F = a2b
        xa = A1.ap()
        pend2 = None
        for tt in range(NO):
            xload(xo_in.ap()[tt * 128:(tt + 1) * 128, :], tt % 2, queue="sp")
            for dm in range(4):
                bko = dm % 2

                def omm1(t, b=bko, tt=tt, dm=dm):
                    last = None
                    for c in range(14):
                        last = t.matmul(
                            bank(b),
                            attnF[:, c * 1024 + tt * 128:c * 1024 + (tt + 1) * 128],
                            wo_chunk(c)[:, dm * 512:(dm + 1) * 512],
                            start=(c == 0), stop=False)
                    return last
                rdl = [attnF[:, c * 1024 + tt * 128:c * 1024 + (tt + 1) * 128]
                       for c in range(14)]
                rdl += [wo_chunk(c) for c in range(14)]
                OP("pe", omm1, reads=rdl, writes=[bank(bko)], label=f"o{tt}_{dm}a")

                def omm2(t, b=bko, tt=tt, dm=dm):
                    last = None
                    for c in range(14, NCH):
                        last = t.matmul(
                            bank(b),
                            attnF[:, c * 1024 + tt * 128:c * 1024 + (tt + 1) * 128],
                            wo_chunk(c)[:, dm * 512:(dm + 1) * 512],
                            start=False, stop=False)
                    last = t.matmul(bank(b), ONE.ap(),
                                    BOR.ap()[:, dm * 512:(dm + 1) * 512],
                                    start=False, stop=True)
                    return last
                rdl = [attnF[:, c * 1024 + tt * 128:c * 1024 + (tt + 1) * 128]
                       for c in range(14, NCH)]
                rdl += [ONE.ap(), BOR.ap()]
                rdl += [wo_chunk(c) for c in range(14, NCH)]
                OP("pe", omm2, reads=rdl, writes=[bank(bko)], label=f"o{tt}_{dm}b")
                xa_dst = xa[:, tt * 2048 + dm * 512:tt * 2048 + (dm + 1) * 512]
                xo_src = xrot[tt % 2][:, dm * 512:(dm + 1) * 512]
                OP("dve", lambda v, d=xa_dst, xs=xo_src, b=bko: v.tensor_add(
                    d, xs, bank(b)),
                   reads=[xo_src, bank(bko)], writes=[xa_dst], label=f"oe{tt}_{dm}")
            xv = xa[:, tt * 2048:(tt + 1) * 2048]
            r = ln_stats(xv, f"l2_{tt}")
            ht = ln_norm(xv, r, f"l2_{tt}")
            if pend2 is not None:
                ln_tail(*pend2)
            pend2 = (ht, h2F, 1024, tt * 128, f"l2_{tt}")
        ln_tail(*pend2)

        B0 = A6.ap()[:, 0:4096]
        B1 = A6.ap()[:, 4096:8192]
        g_hold = [A3.ap()[:, 5632 + i * 512:5632 + (i + 1) * 512] for i in range(4)]
        aF = a3b[:, 0:11264]

        def wchunk(bi, c):
            if bi == 0:
                return B0[:, c * 256:(c + 1) * 256]
            if bi == 1:
                return B1[:, c * 256:(c + 1) * 256]
            if c < 8:
                return PSA.ap()[:, c * 256:(c + 1) * 256]
            if c < 12:
                return PSB.ap()[:, (c - 8) * 256:(c - 7) * 256]
            return PTS.ap()[:, (c - 12) * 256:(c - 11) * 256]

        def wparts(bi):
            if bi == 0:
                return [(B0, 0, 4096)]
            if bi == 1:
                return [(B1, 0, 4096)]
            return [(PSA.ap(), 0, 2048), (PSB.ap(), 2048, 1024),
                    (PTS.ap(), 3072, 1024)]

        def wreads(bi):
            return [p for p, _, _ in wparts(bi)]

        slice_items = []
        grp_slices = []
        for grp in range(4):
            tiles = list(range(grp * 11, grp * 11 + 11))
            done = set()
            lst = []
            for hh2 in tiles:
                s = hh2 // 2
                if s in done:
                    continue
                done.add(s)
                lst.append((s, [t for t in (2 * s, 2 * s + 1) if t in tiles]))
                slice_items.append((wg_in, s, f"wgL{grp}_"))
                slice_items.append((wu_in, s, f"wuL{grp}_"))
            grp_slices.append(lst)

        def emit_load(i):
            if i >= len(slice_items):
                return
            w_in, s, tag = slice_items[i]
            for part, off, wdt in wparts(i % 3):
                wload(part, w_in, s * 4096 + off, wdt, f"{tag}{s}_{off}")
                OP("dve", lambda v, b=part: v.tensor_scalar_add(b, b, -128.0),
                   reads=[part], writes=[part], label=f"{tag}c{s}_{off}")

        emit_load(0)
        emit_load(1)

        def wd_tile(k):
            if k < 4:
                return A5.ap()[:, k * 2048:(k + 1) * 2048]
            if k < 8:
                return a7b[:, (k - 4) * 2048:(k - 3) * 2048]
            if k == 8:
                return WT0.ap()
            if k == 9:
                return WT1.ap()
            return H1T.ap()

        _it = [0]
        for grp in range(4):
            _wdk = [0]

            def emit_wd2(n, grp, st):
                for _ in range(n):
                    if st[0] >= 11:
                        return
                    k = st[0]
                    st[0] += 1
                    hidc = grp * 11 + k
                    dstw = wd_tile(k)
                    wload(dstw, wd_in, hidc * 2048, 2048, f"wdL{grp}_{k}")
                    OP("dve", lambda v, d=dstw: v.tensor_scalar_add(d, d, -128.0),
                       reads=[dstw], writes=[dstw], label=f"wdc{grp}_{k}")
                    OP("dve", lambda v, d=dstw: v.tensor_mul(d, d, sdbap),
                       reads=[dstw, sdbap], writes=[dstw], label=f"wds{grp}_{k}")

            for s, stiles in grp_slices[grp]:
                ig = _it[0]
                _it[0] += 1
                emit_load(ig + 2)
                gi = 0
                for t_ in stiles:
                    col = (t_ % 2) * 128
                    for tthalf in range(2):
                        bkg = gi % 2

                        def gmm(t, b=bkg, col=col, th=tthalf, bi=ig % 3):
                            last = None
                            for c in range(NCH):
                                last = t.matmul(
                                    bank(b),
                                    wchunk(bi, c)[:, col:col + 128],
                                    h2F[:, c * 1024 + th * 512:c * 1024 + (th + 1) * 512],
                                    start=(c == 0), stop=(c == NCH - 1))
                            return last
                        grd = wreads(ig % 3) + [
                            h2F[:, c * 1024 + tthalf * 512:c * 1024 + (tthalf + 1) * 512]
                            for c in range(NCH)]
                        OP("pe", gmm, reads=grd,
                           writes=[bank(bkg)], label=f"g{t_}_{tthalf}")
                        OP("act", lambda a, b=bkg, d=g_hold[gi % 4], hh2=t_: a.activation(
                            d, bank(b), AF.Silu, scale=SGP.ap()[:, hh2:hh2 + 1]),
                           reads=[bank(bkg), SGP.ap()], writes=[g_hold[gi % 4]],
                           label=f"ge{t_}_{tthalf}")
                        gi += 1
                emit_wd2(2, grp, _wdk)
                iu = _it[0]
                _it[0] += 1
                emit_load(iu + 2)
                gi = 0
                for t_ in stiles:
                    col = (t_ % 2) * 128
                    k_ = t_ - grp * 11
                    for tthalf in range(2):
                        bku = 2 + gi % 2

                        def ummf(t, b=bku, col=col, th=tthalf, bi=iu % 3):
                            last = None
                            for c in range(NCH):
                                last = t.matmul(
                                    bank(b),
                                    wchunk(bi, c)[:, col:col + 128],
                                    h2F[:, c * 1024 + th * 512:c * 1024 + (th + 1) * 512],
                                    start=(c == 0), stop=(c == NCH - 1))
                            return last
                        urd = wreads(iu % 3) + [
                            h2F[:, c * 1024 + tthalf * 512:c * 1024 + (tthalf + 1) * 512]
                            for c in range(NCH)]
                        OP("pe", ummf, reads=urd,
                           writes=[bank(bku)], label=f"u{t_}_{tthalf}")
                        adst = aF[:, k_ * 1024 + tthalf * 512:k_ * 1024 + (tthalf + 1) * 512]
                        OP("act", lambda a, b=bku, d=adst, hh2=t_: a.activation(
                            d, bank(b), AF.Identity, scale=SUP.ap()[:, hh2:hh2 + 1]),
                           reads=[bank(bku), SUP.ap()], writes=[adst],
                           label=f"ue{t_}_{tthalf}")
                        OP("dve", lambda v, d=adst, g=g_hold[gi % 4]: v.tensor_mul(
                            d, g, d),
                           reads=[g_hold[gi % 4], adst], writes=[adst],
                           label=f"a{t_}_{tthalf}")
                        gi += 1
            emit_wd2(11, grp, _wdk)
            for tt in range(NO):
                for dm in range(4):
                    bkd = 4 + (dm % 2)

                    def dmm(t, b=bkd, tt=tt, dm=dm):
                        last = None
                        for k in range(11):
                            last = t.matmul(
                                bank(b),
                                aF[:, k * 1024 + tt * 128:k * 1024 + (tt + 1) * 128],
                                wd_tile(k)[:, dm * 512:(dm + 1) * 512],
                                start=(k == 0), stop=(k == 10))
                        return last
                    rdl = [aF] + [wd_tile(k) for k in range(11)]
                    OP("pe", dmm, reads=rdl, writes=[bank(bkd)],
                       label=f"d{grp}_{tt}_{dm}")
                    xv = xa[:, tt * 2048 + dm * 512:tt * 2048 + (dm + 1) * 512]
                    OP("dve", lambda v, x=xv, b=bkd: v.tensor_tensor(
                        x, x, bank(b), ALU.add),
                       reads=[xv, bank(bkd)], writes=[xv], label=f"de{grp}_{tt}_{dm}")

        for tt in range(NO):
            src = xa[:, tt * 2048:(tt + 1) * 2048]
            OP("sp", lambda s, sr=src, tt=tt: s.dma_start(
                out=y_out.ap()[tt * 128:(tt + 1) * 128, :], in_=sr),
               reads=[src], writes=[], label=f"st{tt}")
        for i in range(8):
            se = sch.sems[f"dmas{i}"]
            if se.n:
                sch.raw("sp", lambda s, h=se.h, v=se.n: s.wait_ge(h, v))

        block = E(nc.Block())
        sch.emit(block)

    return nc



def _mask_host(parity):
    d0 = parity * 128
    jj = np.arange(768)[None, :]
    row = np.arange(128)[:, None]
    m = np.where(jj <= row + d0 + 256, 0.0, NEG).astype(np.float32)
    return m.astype(ml_dtypes.bfloat16)


def _shared_weights(inp):
    wq = np.asarray(inp["L0_attn_Wq"])
    wk = np.asarray(inp["L0_attn_Wk"])
    wv = np.asarray(inp["L0_attn_Wv"])
    wo = np.asarray(inp["L0_attn_Wo"])
    wg = np.asarray(inp["L0_ffn_wq_gate"])
    wu = np.asarray(inp["L0_ffn_wq_up"])
    wd = np.asarray(inp["L0_ffn_wq_down"])

    def qkv_pp(w):
        return np.ascontiguousarray(
            w.reshape(8, 256, 16, 128).transpose(3, 0, 2, 1).reshape(128, 32768))

    def gu_pp(w):
        return np.ascontiguousarray(
            w.reshape(22, 256, 16, 128).transpose(3, 0, 2, 1).reshape(128, 90112))

    return {
        "wqp": qkv_pp(wq),
        "wkp": qkv_pp(wk),
        "wvp": qkv_pp(wv),
        "wop": np.ascontiguousarray(
            wo.reshape(2048, 16, 128).transpose(2, 1, 0).reshape(128, 32768)),
        "wgp": gu_pp(wg),
        "wup": gu_pp(wu),
        "wdp": np.ascontiguousarray(
            wd.reshape(2048, 44, 128).transpose(2, 1, 0).reshape(128, 90112)),
        "bq_pp": np.ascontiguousarray(np.asarray(inp["L0_attn_bq"]).reshape(16, 128).T),
        "bk_pp": np.ascontiguousarray(np.asarray(inp["L0_attn_bk"]).reshape(16, 128).T),
        "bv_pp": np.ascontiguousarray(np.asarray(inp["L0_attn_bv"]).reshape(16, 128).T),
        "bo_row": np.ascontiguousarray(np.asarray(inp["L0_attn_bo"]).reshape(1, 2048)),
        "sg_pp": np.ascontiguousarray(np.asarray(inp["L0_ffn_s_gate"]).reshape(44, 128).T),
        "su_pp": np.ascontiguousarray(np.asarray(inp["L0_ffn_s_up"]).reshape(44, 128).T),
        "sdb": np.ascontiguousarray(
            np.broadcast_to(np.asarray(inp["L0_ffn_s_down"]), (128, 2048))),
        "ident": np.eye(128, dtype=np.float32).astype(ml_dtypes.bfloat16),
    }


def _core_inputs(c, inp, shared):
    b, p = c // 2, c % 2
    X = np.asarray(inp["L0_attn_X"])
    rows = np.concatenate([np.arange((p + 2 * j) * 128, (p + 2 * j) * 128 + 128)
                           for j in range(8)])
    d = dict(shared)
    d["x"] = np.ascontiguousarray(X[b])
    d["xown"] = np.ascontiguousarray(X[b][rows])
    d["mask"] = _mask_host(p)
    return d, b, rows


def kernel(**inputs):
    nc = build_nc()
    shared = _shared_weights(inputs)
    in_maps = []
    meta = []
    for c in range(8):
        m, b, rows = _core_inputs(c, inputs, shared)
        in_maps.append(m)
        meta.append((b, rows))
    res = run_bass_kernel_spmd(nc, in_maps, core_ids=list(range(8)))
    out = np.empty((4, 2048, 2048), dtype=np.float32)
    for c in range(8):
        b, rows = meta[c]
        out[b][rows] = np.asarray(res.results[c]["y"], dtype=np.float32)
    return out


# version 24 (final)
# speedup vs baseline: 1.0797x; 1.0084x over previous
import numpy as np
import ml_dtypes
from contextlib import ExitStack
import concourse.bass as bass
import concourse.mybir as mybir
from concourse.bass_utils import run_bass_kernel_spmd

F32 = mybir.dt.float32
BF16 = mybir.dt.bfloat16
I32 = mybir.dt.int32
AF = mybir.ActivationFunctionType
ALU = mybir.AluOpType
AX = mybir.AxisListType

D = 2048
SEQ = 2048
H = 16
NCH = 16
NT = 16
NO = 8
NHT = 44
HID = 5632
SCL = float(128.0 ** -0.25)
EPS = 1e-5
RJ = [1, 1, 2, 2, 3, 3, 4, 4]
QORD = [7, 0, 6, 1, 5, 2, 4, 3]
LSS = [0, 1, 42, 43]
RCS = [2, 3, 40, 41]
NEG = -1e30

_SB_GRAN = 256
_PS_GRAN = 2048
_DR_GRAN = 1 << 18


class _Sem:
    __slots__ = ("name", "n", "snaps", "h")

    def __init__(self, name):
        self.name = name
        self.n = 0
        self.snaps = {}
        self.h = None


class _Sched:
    def __init__(self):
        self.streams = {k: [] for k in ("pe", "dve", "act", "pool", "sp")}
        self.know = {k: {} for k in self.streams}
        self.slots = {}
        self.sems = {k: _Sem(k) for k in ("pe", "dve", "act")}
        for i in range(8):
            self.sems[f"dmag{i}"] = _Sem(f"dmag{i}")
            self.sems[f"dmas{i}"] = _Sem(f"dmas{i}")
        self.rot = {"pool": 0, "sp": 0}

    def _keys(self, ap):
        t = ap.tensor
        tn = type(t).__name__
        esz = mybir.dt.size(ap.dtype)
        dims = [list(d) for d in ap.ap]
        name = t.name
        if tn == "DRamTensorHandle":
            span = 1 + sum(s * (c - 1) for s, c in dims)
            lo, hi = ap.offset * esz, (ap.offset + span) * esz
            gran, psum = _DR_GRAN, False
        else:
            psum = tn == "PSumTensorHandle"
            pstride = dims[0][0] if dims else 1
            off = ap.offset % pstride if pstride > 0 else ap.offset
            free = dims[1:]
            span = 1 + sum(s * (c - 1) for s, c in free) if free else 1
            lo, hi = off * esz, (off + span) * esz
            gran = 4 if name == "SM" else (_PS_GRAN if psum else _SB_GRAN)
        return {(name, b) for b in range(lo // gran, (hi - 1) // gran + 1)}, psum

    def op(self, eng, build, reads=(), writes=(), label=""):
        if eng in ("pool", "sp"):
            pfx = "dmag" if eng == "pool" else "dmas"
            sem = self.sems[f"{pfx}{self.rot[eng] % 8}"]
            self.rot[eng] += 1
            amt = 16
        else:
            sem = self.sems[eng]
            amt = 1
        waits = {}
        if amt == 16 and sem.n > 0:
            waits[sem] = sem.n

        def addw(sv):
            if sv is None:
                return
            s, v = sv
            if eng == "pe" and s is self.sems["pe"]:
                return
            if waits.get(s, -1) < v:
                waits[s] = v

        rd, wr = set(), set()
        for ap in reads:
            k, psum = self._keys(ap)
            (wr if psum else rd).update(k)
        for ap in writes:
            k, _ = self._keys(ap)
            wr.update(k)
        rd -= wr
        for k in rd:
            sl = self.slots.get(k)
            if sl:
                addw(sl["w"])
        for k in wr:
            sl = self.slots.get(k)
            if sl:
                addw(sl["w"])
                for r in sl["rs"]:
                    addw(r)
        kn = self.know[eng]
        real = [(s, v) for s, v in waits.items() if kn.get(s, 0) < v]
        sem.n += amt
        val = sem.n
        for s, v in real:
            if kn.get(s, 0) < v:
                kn[s] = v
            snap = s.snaps.get(v)
            if snap:
                for s2, v2 in snap.items():
                    if kn.get(s2, 0) < v2:
                        kn[s2] = v2
        pub = dict(kn)
        pub[sem] = val
        sem.snaps[val] = pub
        for k in wr:
            self.slots[k] = {"w": (sem, val), "rs": []}
        for k in rd:
            sl = self.slots.setdefault(k, {"w": None, "rs": []})
            sl["rs"].append((sem, val))
        self.streams[eng].append((real, build, sem, amt, label))

    def raw(self, eng, build):
        self.streams[eng].append(([], build, None, 0, "raw"))

    def emit(self, block):
        deco = {"pe": block.tensor, "dve": block.vector, "act": block.scalar,
                "pool": block.gpsimd, "sp": block.sync}

        def runner(stream):
            def run(eng):
                for waits, build, sem, amt, label in stream:
                    for s, v in waits:
                        eng.wait_ge(s.h, v)
                    inst = build(eng)
                    if sem is not None:
                        inst.then_inc(sem.h, amt)
            return run

        for name in self.streams:
            deco[name](runner(self.streams[name]))


def build_nc():
    nc = bass.Bass("TRN2", target_bir_lowering=False)

    x_in = nc.dram_tensor("x", [SEQ, D], F32, kind="ExternalInput")
    xo_in = nc.dram_tensor("xown", [1024, D], F32, kind="ExternalInput")
    wq_in = nc.dram_tensor("wqp", [128, 8 * 4096], F32, kind="ExternalInput")
    wk_in = nc.dram_tensor("wkp", [128, 8 * 4096], F32, kind="ExternalInput")
    wv_in = nc.dram_tensor("wvp", [128, 8 * 4096], F32, kind="ExternalInput")
    wo_in = nc.dram_tensor("wop", [128, 16 * 2048], F32, kind="ExternalInput")
    bq_in = nc.dram_tensor("bq_pp", [128, H], F32, kind="ExternalInput")
    bk_in = nc.dram_tensor("bk_pp", [128, H], F32, kind="ExternalInput")
    bv_in = nc.dram_tensor("bv_pp", [128, H], F32, kind="ExternalInput")
    bo_in = nc.dram_tensor("bo_row", [1, D], F32, kind="ExternalInput")
    wg_in = nc.dram_tensor("wgp", [128, 22 * 4096], I32, kind="ExternalInput")
    wu_in = nc.dram_tensor("wup", [128, 22 * 4096], I32, kind="ExternalInput")
    wd_in = nc.dram_tensor("wdp", [128, 44 * 2048], I32, kind="ExternalInput")
    sg_in = nc.dram_tensor("sg_pp", [128, NHT], F32, kind="ExternalInput")
    su_in = nc.dram_tensor("su_pp", [128, NHT], F32, kind="ExternalInput")
    sdb_in = nc.dram_tensor("sdb", [128, D], F32, kind="ExternalInput")
    mask_in = nc.dram_tensor("mask", [128, 768], BF16, kind="ExternalInput")
    id_in = nc.dram_tensor("ident", [128, 128], BF16, kind="ExternalInput")
    y_out = nc.dram_tensor("y", [1024, D], F32, kind="ExternalOutput")

    with ExitStack() as ctx:
        E = ctx.enter_context
        A1 = E(nc.sbuf_tensor("A1", [128, 16384], F32))
        A2 = E(nc.sbuf_tensor("A2", [128, 8192], F32))
        A3 = E(nc.sbuf_tensor("A3", [128, 8192], F32))
        A5 = E(nc.sbuf_tensor("A5", [128, 8192], BF16))
        A6 = E(nc.sbuf_tensor("A6", [128, 8192], BF16))
        A7 = E(nc.sbuf_tensor("A7", [128, 4096], F32))
        WT0 = E(nc.sbuf_tensor("WT0", [128, 2048], BF16))
        WT1 = E(nc.sbuf_tensor("WT1", [128, 2048], BF16))
        H1T = E(nc.sbuf_tensor("H1T", [128, 2048], BF16))
        PSA = E(nc.sbuf_tensor("PSA", [128, 2048], BF16))
        PSB = E(nc.sbuf_tensor("PSB", [128, 1024], BF16))
        PTS = E(nc.sbuf_tensor("PTS", [128, 1024], BF16))
        MSK = E(nc.sbuf_tensor("MSK", [128, 768], BF16))
        IDT = E(nc.sbuf_tensor("IDT", [128, 128], BF16))
        BOR = E(nc.sbuf_tensor("BOR", [1, D], BF16))
        ONE = E(nc.sbuf_tensor("ONE", [1, 128], BF16))
        SGP = E(nc.sbuf_tensor("SGP", [128, NHT], F32))
        SUP = E(nc.sbuf_tensor("SUP", [128, NHT], F32))
        SDB = E(nc.sbuf_tensor("SDB", [128, D], BF16))
        SM = E(nc.sbuf_tensor("SM", [128, 48], F32))
        BNS = E(nc.sbuf_tensor("BNS", [128, 48], F32))
        BQS = E(nc.sbuf_tensor("BQS", [128, H], F32))
        BKS = E(nc.sbuf_tensor("BKS", [128, H], F32))
        BVP = E(nc.sbuf_tensor("BVP", [128, H], F32))
        PS = E(nc.psum_tensor("PS", [128, 4096], F32))

        sch = _Sched()
        for k, s in sch.sems.items():
            s.h = E(nc.semaphore(k))

        a1b = A1.ap().bitcast(BF16)
        a2b = A2.ap().bitcast(BF16)
        a3b = A3.ap().bitcast(BF16)
        a7b = A7.ap().bitcast(BF16)
        psf = PS.ap()
        psb = PS.ap().bitcast(BF16)
        sdbap = SDB.ap()

        def bank(i, n=1):
            return psf[:, i * 512:(i + n) * 512]

        def bankw(i, w):
            return psf[:, i * 512:i * 512 + w]

        def bank16(i):
            return psb[:, i * 1024:(i + 1) * 1024]

        def sm(c):
            return SM.ap()[:, c:c + 1]

        OP = sch.op

        def ld(dst, src, label=""):
            OP("pool", lambda g, d=dst, s=src: g.dma_start(out=d, in_=s),
               reads=[src], writes=[dst], label=label)

        ld(IDT.ap(), id_in.ap(), "ident")
        ld(MSK.ap(), mask_in.ap(), "mask")
        ld(BOR.ap(), bo_in.ap(), "bo")
        ld(SGP.ap(), sg_in.ap(), "sg")
        ld(SUP.ap(), su_in.ap(), "su")
        ld(SDB.ap(), sdb_in.ap(), "sdb")
        ld(BQS.ap(), bq_in.ap(), "bq")
        ld(BKS.ap(), bk_in.ap(), "bk")
        ld(BVP.ap(), bv_in.ap(), "bv")
        OP("dve", lambda v: v.memset(ONE.ap(), 1.0), writes=[ONE.ap()], label="ones")
        OP("dve", lambda v: v.memset(sm(22), EPS), writes=[sm(22)], label="eps")
        OP("dve", lambda v: v.tensor_scalar_mul(BQS.ap(), BQS.ap(), SCL),
           reads=[BQS.ap()], writes=[BQS.ap()], label="bqs")
        OP("dve", lambda v: v.tensor_scalar_mul(BKS.ap(), BKS.ap(), SCL),
           reads=[BKS.ap()], writes=[BKS.ap()], label="bks")

        xrot = [A7.ap()[:, 0:2048], A7.ap()[:, 2048:4096]]

        def wload(dst, w_in, c0, w, label=""):
            src = w_in.ap()[:, c0:c0 + w]
            OP("pool", lambda g, d=dst, s=src: g.dma_start(out=d, in_=s),
               reads=[src], writes=[dst], label=label)

        def xload(src, r, queue="sp"):
            dst = xrot[r]
            OP(queue, lambda g, d=dst, s=src: g.dma_start(out=d, in_=s),
               reads=[src], writes=[dst], label="xld")

        _ln = [0]

        def ln_stats(xap, label):
            r = _ln[0] % 2
            _ln[0] += 1
            bns = BNS.ap()[:, r * 24:(r + 1) * 24]
            for ch in range(4):
                OP("dve", lambda v, o=bns[:, ch * 6:(ch + 1) * 6],
                   x=xap[:, ch * 512:(ch + 1) * 512]: v.bn_stats(o, x),
                   reads=[xap[:, ch * 512:(ch + 1) * 512]],
                   writes=[bns[:, ch * 6:(ch + 1) * 6]], label=label + f"s{ch}")
            c_mv, c_std, c_rstd = 8 + 2 * r, 16 + r, 18 + r
            mv = SM.ap()[:, c_mv:c_mv + 2]
            OP("dve", lambda v, o=mv, s=bns: v.bn_aggr(o, s),
               reads=[bns], writes=[mv], label=label + "a")
            OP("act", lambda a: a.activation(sm(c_std), sm(c_mv + 1), AF.Sqrt,
                                             bias=sm(22)),
               reads=[sm(c_mv + 1), sm(22)], writes=[sm(c_std)], label=label + "d")
            OP("dve", lambda v: v.reciprocal(sm(c_rstd), sm(c_std)),
               reads=[sm(c_std)], writes=[sm(c_rstd)], label=label + "r")
            return r

        def ln_norm(xap, r, label):
            c_mean, c_rstd = 8 + 2 * r, 18 + r
            htile = H1T.ap() if r == 0 else PSA.ap()
            for hf2 in range(2):
                OP("dve", lambda v, x=xap[:, hf2 * 1024:(hf2 + 1) * 1024],
                   o=htile[:, hf2 * 1024:(hf2 + 1) * 1024]: v.tensor_scalar(
                    o, x, sm(c_mean), sm(c_rstd), ALU.subtract, ALU.mult),
                   reads=[xap[:, hf2 * 1024:(hf2 + 1) * 1024], sm(c_mean),
                          sm(c_rstd)],
                   writes=[htile[:, hf2 * 1024:(hf2 + 1) * 1024]],
                   label=label + f"n{hf2}")
            return htile

        def ln_tail(htile, outFb, cstride, tcol, label):
            for half in range(2):
                bkx = 6 + half

                def tr(t, h=htile, b=bkx, hf=half):
                    last = None
                    for c in range(8):
                        last = t.transpose(
                            bank16(b)[:, c * 128:(c + 1) * 128],
                            h[:, (hf * 8 + c) * 128:(hf * 8 + c + 1) * 128],
                            IDT.ap())
                    return last

                OP("pe", tr,
                   reads=[htile[:, half * 1024:(half + 1) * 1024], IDT.ap()],
                   writes=[bank16(bkx)], label=label + f"T{half}")
                seg = outFb[:, half * 8 * cstride:(half * 8 + 8) * cstride]
                dst = seg.rearrange("p (a b) -> p a b", a=8, b=cstride)[:, :, tcol:tcol + 128]
                OP("act", lambda a, b=bkx, d=dst: a.activation(
                    d, bank16(b)[:, 0:1024], AF.Identity),
                   reads=[bank16(bkx)], writes=[dst], label=label + f"E{half}")

        h1F = a1b
        h1oF = a2b
        QF = a3b

        a5f = A5.ap().bitcast(F32)
        xrot4 = [A7.ap()[:, 0:2048], A7.ap()[:, 2048:4096],
                 a5f[:, 0:2048], a5f[:, 2048:4096]]

        def xload4(src, r):
            dst = xrot4[r]
            OP("sp", lambda g, d=dst, s=src: g.dma_start(out=d, in_=s),
               reads=[src], writes=[dst], label="xld")

        wqbuf = [A6.ap()[:, 0:4096], A6.ap()[:, 4096:8192]]

        def emit_q(pr):
            buf = wqbuf[pr % 2]
            wload(buf, wq_in, pr * 4096, 4096, f"wqL{pr}")
            for hp in range(2):
                hh = pr * 2 + hp
                for half in range(2):
                    bkq = half

                    def qmm(t, b=bkq, bu=buf, hp=hp, hf=half):
                        last = None
                        for c in range(NCH):
                            last = t.matmul(
                                bank(b),
                                bu[:, c * 256 + hp * 128:c * 256 + (hp + 1) * 128],
                                h1oF[:, c * 1024 + hf * 512:c * 1024 + (hf + 1) * 512],
                                start=(c == 0), stop=(c == NCH - 1))
                        return last
                    qrd = [buf] + [h1oF[:, c * 1024 + half * 512:c * 1024 + (half + 1) * 512]
                                   for c in range(NCH)]
                    OP("pe", qmm, reads=qrd, writes=[bank(bkq)],
                       label=f"q{hh}_{half}")
                    dst = QF[:, hh * 1024 + half * 512:hh * 1024 + (half + 1) * 512]
                    OP("act", lambda a, b=bkq, d=dst, hh=hh: a.activation(
                        d, bank(b), AF.Identity, bias=BQS.ap()[:, hh:hh + 1],
                        scale=SCL),
                       reads=[bank(bkq), BQS.ap()], writes=[dst], label=f"qe{hh}")

        ln_jobs = [(xo_in.ap()[i * 128:(i + 1) * 128, :], h1oF, 1024, i * 128, f"lo_{i}")
                   for i in range(NO)]
        ln_jobs += [(x_in.ap()[i * 128:(i + 1) * 128, :], h1F, 2048, i * 128, f"l1_{i}")
                    for i in range(NT)]
        xload4(ln_jobs[0][0], 0)
        xload4(ln_jobs[1][0], 1)
        pend = None
        for i, (src, outFb, cs, tc, lab) in enumerate(ln_jobs):
            r = ln_stats(xrot4[i % 4], lab)
            ht = ln_norm(xrot4[i % 4], r, lab)
            if i + 2 < len(ln_jobs):
                xload4(ln_jobs[i + 2][0], (i + 2) % 4)
            if pend is not None:
                ln_tail(*pend)
            pend = (ht, outFb, cs, tc, lab)
            if 9 <= i <= 16:
                emit_q(i - 9)
        ln_tail(*pend)

        wvbuf = A6.ap()[:, 0:4096]
        wkbuf = A6.ap()[:, 4096:8192]
        VT = A5.ap()[:, 0:4096]
        kbuf = [A5.ap()[:, 4096:6144], A5.ap()[:, 6144:8192]]
        PTS4 = [PTS.ap()[:, 0:512], PTS.ap()[:, 512:1024],
                H1T.ap()[:, 512:1024], H1T.ap()[:, 1536:2048]]
        woA = A6.ap()
        woB = a3b
        wload(WT0.ap(), wo_in, 24576, 2048, "woC12")
        wload(WT1.ap(), wo_in, 26624, 2048, "woC13")

        for pr in range(8):
            if pr == 0:
                wload(wvbuf, wv_in, 0, 4096, "wvL0")
            for kt in range(NT):
                bkv = kt % 4

                def vmm(t, b=bkv, kt=kt):
                    last = None
                    for c in range(NCH):
                        last = t.matmul(
                            bankw(b, 256),
                            h1F[:, c * 2048 + kt * 128:c * 2048 + (kt + 1) * 128],
                            wvbuf[:, c * 256:(c + 1) * 256],
                            start=(c == 0), stop=(c == NCH - 1))
                    return last
                vrd = [wvbuf] + [h1F[:, c * 2048 + kt * 128:c * 2048 + (kt + 1) * 128]
                                 for c in range(NCH)]
                OP("pe", vmm, reads=vrd, writes=[bank(bkv)],
                   label=f"v{pr}_{kt}")
                dst = VT[:, kt * 256:(kt + 1) * 256]
                OP("act", lambda a, b=bkv, d=dst: a.activation(d, bankw(b, 256),
                                                               AF.Identity),
                   reads=[bank(bkv)], writes=[dst], label=f"ve{pr}_{kt}")
            if pr < 7:
                wload(wvbuf, wv_in, (pr + 1) * 4096, 4096, f"wvL{pr + 1}")
            if pr == 0:
                wload(wkbuf, wk_in, 0, 4096, "wkL0")
            if pr == 7:
                wload(woA[:, 0:4096], wo_in, 0, 4096, "woA0")

            def kproj_kt(hp2, kt, prv=pr):
                hh2 = prv * 2 + hp2
                kb2 = kbuf[hp2]

                def kmm(t, kt=kt, hp2=hp2):
                    last = None
                    for c in range(NCH):
                        last = t.matmul(
                            bank(6),
                            wkbuf[:, c * 256 + hp2 * 128:c * 256 + (hp2 + 1) * 128],
                            h1F[:, c * 2048 + kt * 512:c * 2048 + (kt + 1) * 512],
                            start=(c == 0), stop=(c == NCH - 1))
                    return last
                krd = [wkbuf] + [h1F[:, c * 2048 + kt * 512:c * 2048 + (kt + 1) * 512]
                                 for c in range(NCH)]
                OP("pe", kmm, reads=krd, writes=[bank(6)],
                   label=f"k{hh2}_{kt}")
                dst = kb2[:, kt * 512:(kt + 1) * 512]
                OP("act", lambda a, d=dst, hh2=hh2: a.activation(
                    d, bank(6), AF.Identity, bias=BKS.ap()[:, hh2:hh2 + 1],
                    scale=SCL),
                   reads=[bank(6), BKS.ap()], writes=[dst], label=f"ke{hh2}")

            for hp in range(2):
                hh = pr * 2 + hp
                kb = kbuf[hp]
                if hp == 0:
                    if pr == 0:
                        for kt in range(4):
                            kproj_kt(0, kt)
                elif pr == 7:
                    wload(woA[:, 4096:8192], wo_in, 4096, 4096, "woA1")
                    wload(A5.ap()[:, 4096:6144], wo_in, 28672, 2048, "woC14")
                else:
                    wload(wkbuf, wk_in, (pr + 1) * 4096, 4096, f"wkL{pr + 1}")

                def s_group(idx, j):
                    R = RJ[j]
                    base = 0 if idx % 2 == 0 else 4
                    moff = 256 if j % 2 == 0 else 0
                    r2 = idx % 2
                    for rr in range(R):
                        lastb = rr == R - 1

                        def smm(t, rr=rr, base=base, j=j, hh=hh, moff=moff,
                                kb=kb, lastb=lastb):
                            last = t.matmul(
                                bank(base + rr),
                                QF[:, hh * 1024 + j * 128:hh * 1024 + (j + 1) * 128],
                                kb[:, rr * 512:(rr + 1) * 512],
                                start=True, stop=not lastb)
                            if lastb:
                                last = t.matmul(bank(base + rr), IDT.ap(),
                                                MSK.ap()[:, moff:moff + 512],
                                                start=False, stop=True)
                            return last
                        OP("pe", smm,
                           reads=[QF[:, hh * 1024 + j * 128:hh * 1024 + (j + 1) * 128],
                                  kb[:, rr * 512:(rr + 1) * 512], IDT.ap(), MSK.ap()],
                           writes=[bank(base + rr)], label=f"s{hh}_{j}_{rr}")
                        OP("dve", lambda v, rr=rr, base=base, r2=r2: v.reduce_max(
                            sm(24 + 4 * r2 + rr), bank(base + rr), axis=AX.X),
                           reads=[bank(base + rr)],
                           writes=[sm(24 + 4 * r2 + rr)], label=f"mxp{hh}_{j}_{rr}")

                def softmax_max(idx, j):
                    R = RJ[j]
                    r2 = idx % 2
                    src = SM.ap()[:, 24 + 4 * r2:24 + 4 * r2 + R]
                    OP("dve", lambda v, sr=src, r2=r2: v.reduce_max(
                        sm(4 + r2), sr, axis=AX.X, negate=True),
                       reads=[src], writes=[sm(4 + r2)], label=f"mx{hh}_{j}")

                def softmax_exp(idx, j):
                    R = RJ[j]
                    base = 0 if idx % 2 == 0 else 4
                    r2 = idx % 2
                    P = PSA.ap() if base == 0 else PSB.ap()
                    for rr in range(R):
                        psv = psf[:, (base + rr) * 512:(base + rr + 1) * 512]
                        Pv = P[:, rr * 512:(rr + 1) * 512]
                        OP("act", lambda a, p=psv, o=Pv, r2=r2, rr=rr: a.activation(
                            o, p, AF.Exp, bias=sm(4 + r2), scale=1.0,
                            accum_out=sm(32 + 4 * r2 + rr)),
                           reads=[psv, sm(4 + r2)],
                           writes=[Pv, sm(32 + 4 * r2 + rr)], label=f"ex{hh}_{j}_{rr}")
                    ssrc = SM.ap()[:, 32 + 4 * r2:32 + 4 * r2 + R]
                    lsc, rcc = LSS[idx % 4], RCS[idx % 4]
                    OP("dve", lambda v, sr=ssrc, c=lsc: v.reduce_sum(
                        sm(c), sr, axis=AX.X),
                       reads=[ssrc], writes=[sm(lsc)], label=f"ls{hh}_{j}")
                    OP("dve", lambda v, c1=rcc, c0=lsc: v.reciprocal(sm(c1), sm(c0)),
                       reads=[sm(lsc)], writes=[sm(rcc)], label=f"rc{hh}_{j}")

                def ptav_tr(idx, j):
                    R = RJ[j]
                    base = 0 if idx % 2 == 0 else 4
                    P = PSA.ap() if base == 0 else PSB.ap()
                    pds = []
                    for rr in range(R):
                        pb = base + rr

                        def trp(t, pb=pb, rr=rr, P=P):
                            last = None
                            for sub in range(4):
                                last = t.transpose(
                                    bank16(pb)[:, sub * 128:(sub + 1) * 128],
                                    P[:, (rr * 4 + sub) * 128:(rr * 4 + sub + 1) * 128],
                                    IDT.ap())
                            return last
                        OP("pe", trp, reads=[P[:, rr * 512:(rr + 1) * 512], IDT.ap()],
                           writes=[bank16(pb)], label=f"pt{hh}_{j}_{rr}")
                        pdst = PTS4[_seq[0] % 4]
                        if _seq[0] % 2 == 0:
                            OP("dve", lambda v, pb=pb, d=pdst: v.tensor_copy(
                                d, bank16(pb)[:, 0:512]),
                               reads=[bank16(pb)], writes=[pdst],
                               label=f"pv{hh}_{j}_{rr}")
                        else:
                            OP("act", lambda a, pb=pb, d=pdst: a.activation(
                                d, bank16(pb)[:, 0:512], AF.Identity),
                               reads=[bank16(pb)], writes=[pdst],
                               label=f"pv{hh}_{j}_{rr}")
                        _seq[0] += 1
                        pds.append(pdst)
                    return pds

                def ptav_av(idx, j, pds):
                    R = RJ[j]
                    r2 = idx % 2
                    avb = psf[:, 7 * 512 + r2 * 128:7 * 512 + (r2 + 1) * 128]
                    for rr in range(R):
                        pdst = pds[rr]

                        def avm(t, rr=rr, R=R, pdst=pdst, hp=hp, avb=avb):
                            last = None
                            for sub in range(4):
                                kk = rr * 4 + sub
                                last = t.matmul(
                                    avb,
                                    pdst[:, sub * 128:(sub + 1) * 128],
                                    VT[:, kk * 256 + hp * 128:kk * 256 + (hp + 1) * 128],
                                    start=(kk == 0), stop=(kk == R * 4 - 1))
                            return last
                        OP("pe", avm, reads=[pdst, VT], writes=[avb],
                           label=f"av{hh}_{j}_{rr}")

                _seq = [0]

                def evac_ae(idx, j):
                    r2 = idx % 2
                    avb = psf[:, 7 * 512 + r2 * 128:7 * 512 + (r2 + 1) * 128]
                    tmp = H1T.ap()[:, r2 * 1024:r2 * 1024 + 128]
                    rcc = RCS[idx % 4]
                    OP("act", lambda a, t=tmp, av=avb, c=rcc: a.activation(
                        t, av, AF.Identity, scale=sm(c)),
                       reads=[avb, sm(rcc)], writes=[tmp], label=f"ae{hh}_{j}")

                def evac_at(idx, j):
                    r2 = idx % 2
                    tmp = H1T.ap()[:, r2 * 1024:r2 * 1024 + 128]
                    ats = psb[:, 7 * 1024 + 512 + r2 * 128:7 * 1024 + 512 + (r2 + 1) * 128]
                    OP("pe", lambda t, tm=tmp, d=ats: t.transpose(d, tm, IDT.ap()),
                       reads=[tmp, IDT.ap()], writes=[ats], label=f"at{hh}_{j}")
                    adst = a2b[:, hh * 1024 + j * 128:hh * 1024 + (j + 1) * 128]
                    OP("act", lambda a, d=adst, s=ats, hh=hh: a.activation(
                        d, s, AF.Identity, bias=BVP.ap()[:, hh:hh + 1]),
                       reads=[ats, BVP.ap()], writes=[adst], label=f"af{hh}_{j}")

                prev = None
                prev2 = None
                for idx, j in enumerate(QORD):
                    s_group(idx, j)
                    softmax_max(idx, j)
                    if prev2 is not None:
                        evac_ae(*prev2)
                    pds = ptav_tr(*prev) if prev is not None else None
                    softmax_exp(idx, j)
                    if prev2 is not None:
                        evac_at(*prev2)
                    if hp == 0 and idx >= 4:
                        kproj_kt(1, idx - 4)
                    if hp == 1 and pr < 7 and idx >= 4:
                        kproj_kt(0, idx - 4, prv=pr + 1)
                    if prev is not None:
                        ptav_av(*prev, pds)
                    prev2, prev = prev, (idx, j)
                evac_ae(*prev2)
                pds = ptav_tr(*prev)
                evac_at(*prev2)
                ptav_av(*prev, pds)
                evac_ae(*prev)
                evac_at(*prev)
            wload(woB[:, pr * 2048:(pr + 1) * 2048], wo_in, 8192 + pr * 2048,
                  2048, f"woB{pr}")

        attnF = a2b

        def wo_chunk(c):
            if c < 4:
                return woA[:, c * 2048:(c + 1) * 2048]
            if c < 12:
                return woB[:, (c - 4) * 2048:(c - 3) * 2048]
            if c == 12:
                return WT0.ap()
            if c == 13:
                return WT1.ap()
            return A5.ap()[:, (c - 12) * 2048:(c - 11) * 2048]

        wload(A5.ap()[:, 6144:8192], wo_in, 30720, 2048, "woC15")

        h2F = a2b
        xa = A1.ap()
        pend2 = None
        for tt in range(NO):
            xload(xo_in.ap()[tt * 128:(tt + 1) * 128, :], tt % 2, queue="sp")
            for dm in range(4):
                bko = dm % 2

                def omm1(t, b=bko, tt=tt, dm=dm):
                    last = None
                    for c in range(14):
                        last = t.matmul(
                            bank(b),
                            attnF[:, c * 1024 + tt * 128:c * 1024 + (tt + 1) * 128],
                            wo_chunk(c)[:, dm * 512:(dm + 1) * 512],
                            start=(c == 0), stop=False)
                    return last
                rdl = [attnF[:, c * 1024 + tt * 128:c * 1024 + (tt + 1) * 128]
                       for c in range(14)]
                rdl += [wo_chunk(c) for c in range(14)]
                OP("pe", omm1, reads=rdl, writes=[bank(bko)], label=f"o{tt}_{dm}a")

                def omm2(t, b=bko, tt=tt, dm=dm):
                    last = None
                    for c in range(14, NCH):
                        last = t.matmul(
                            bank(b),
                            attnF[:, c * 1024 + tt * 128:c * 1024 + (tt + 1) * 128],
                            wo_chunk(c)[:, dm * 512:(dm + 1) * 512],
                            start=False, stop=False)
                    last = t.matmul(bank(b), ONE.ap(),
                                    BOR.ap()[:, dm * 512:(dm + 1) * 512],
                                    start=False, stop=True)
                    return last
                rdl = [attnF[:, c * 1024 + tt * 128:c * 1024 + (tt + 1) * 128]
                       for c in range(14, NCH)]
                rdl += [ONE.ap(), BOR.ap()]
                rdl += [wo_chunk(c) for c in range(14, NCH)]
                OP("pe", omm2, reads=rdl, writes=[bank(bko)], label=f"o{tt}_{dm}b")
                xa_dst = xa[:, tt * 2048 + dm * 512:tt * 2048 + (dm + 1) * 512]
                xo_src = xrot[tt % 2][:, dm * 512:(dm + 1) * 512]
                OP("dve", lambda v, d=xa_dst, xs=xo_src, b=bko: v.tensor_add(
                    d, xs, bank(b)),
                   reads=[xo_src, bank(bko)], writes=[xa_dst], label=f"oe{tt}_{dm}")
            xv = xa[:, tt * 2048:(tt + 1) * 2048]
            r = ln_stats(xv, f"l2_{tt}")
            ht = ln_norm(xv, r, f"l2_{tt}")
            if pend2 is not None:
                ln_tail(*pend2)
            pend2 = (ht, h2F, 1024, tt * 128, f"l2_{tt}")
        ln_tail(*pend2)

        B0 = A6.ap()[:, 0:4096]
        B1 = A6.ap()[:, 4096:8192]
        g_hold = [A3.ap()[:, 5632 + i * 512:5632 + (i + 1) * 512] for i in range(4)]
        aF = a3b[:, 0:11264]

        def wchunk(bi, c):
            if bi == 0:
                return B0[:, c * 256:(c + 1) * 256]
            if bi == 1:
                return B1[:, c * 256:(c + 1) * 256]
            if c < 8:
                return PSA.ap()[:, c * 256:(c + 1) * 256]
            if c < 12:
                return PSB.ap()[:, (c - 8) * 256:(c - 7) * 256]
            return PTS.ap()[:, (c - 12) * 256:(c - 11) * 256]

        def wparts(bi):
            if bi == 0:
                return [(B0, 0, 4096)]
            if bi == 1:
                return [(B1, 0, 4096)]
            return [(PSA.ap(), 0, 2048), (PSB.ap(), 2048, 1024),
                    (PTS.ap(), 3072, 1024)]

        def wreads(bi):
            return [p for p, _, _ in wparts(bi)]

        slice_items = []
        grp_slices = []
        for grp in range(4):
            tiles = list(range(grp * 11, grp * 11 + 11))
            done = set()
            lst = []
            for hh2 in tiles:
                s = hh2 // 2
                if s in done:
                    continue
                done.add(s)
                lst.append((s, [t for t in (2 * s, 2 * s + 1) if t in tiles]))
                slice_items.append((wg_in, s, f"wgL{grp}_"))
                slice_items.append((wu_in, s, f"wuL{grp}_"))
            grp_slices.append(lst)

        def emit_load(i):
            if i >= len(slice_items):
                return
            w_in, s, tag = slice_items[i]
            for part, off, wdt in wparts(i % 3):
                wload(part, w_in, s * 4096 + off, wdt, f"{tag}{s}_{off}")
                OP("dve", lambda v, b=part: v.tensor_scalar_add(b, b, -128.0),
                   reads=[part], writes=[part], label=f"{tag}c{s}_{off}")

        emit_load(0)
        emit_load(1)

        def wd_tile(k):
            if k < 4:
                return A5.ap()[:, k * 2048:(k + 1) * 2048]
            if k < 8:
                return a7b[:, (k - 4) * 2048:(k - 3) * 2048]
            if k == 8:
                return WT0.ap()
            if k == 9:
                return WT1.ap()
            return H1T.ap()

        _it = [0]
        for grp in range(4):
            _wdk = [0]

            def emit_wd2(n, grp, st):
                for _ in range(n):
                    if st[0] >= 11:
                        return
                    k = st[0]
                    st[0] += 1
                    hidc = grp * 11 + k
                    dstw = wd_tile(k)
                    wload(dstw, wd_in, hidc * 2048, 2048, f"wdL{grp}_{k}")
                    OP("dve", lambda v, d=dstw: v.tensor_scalar_add(d, d, -128.0),
                       reads=[dstw], writes=[dstw], label=f"wdc{grp}_{k}")
                    OP("dve", lambda v, d=dstw: v.tensor_mul(d, d, sdbap),
                       reads=[dstw, sdbap], writes=[dstw], label=f"wds{grp}_{k}")

            for s, stiles in grp_slices[grp]:
                ig = _it[0]
                _it[0] += 1
                emit_load(ig + 2)
                gi = 0
                for t_ in stiles:
                    col = (t_ % 2) * 128
                    for tthalf in range(2):
                        bkg = gi % 2

                        def gmm(t, b=bkg, col=col, th=tthalf, bi=ig % 3):
                            last = None
                            for c in range(NCH):
                                last = t.matmul(
                                    bank(b),
                                    wchunk(bi, c)[:, col:col + 128],
                                    h2F[:, c * 1024 + th * 512:c * 1024 + (th + 1) * 512],
                                    start=(c == 0), stop=(c == NCH - 1))
                            return last
                        grd = wreads(ig % 3) + [
                            h2F[:, c * 1024 + tthalf * 512:c * 1024 + (tthalf + 1) * 512]
                            for c in range(NCH)]
                        OP("pe", gmm, reads=grd,
                           writes=[bank(bkg)], label=f"g{t_}_{tthalf}")
                        OP("act", lambda a, b=bkg, d=g_hold[gi % 4], hh2=t_: a.activation(
                            d, bank(b), AF.Silu, scale=SGP.ap()[:, hh2:hh2 + 1]),
                           reads=[bank(bkg), SGP.ap()], writes=[g_hold[gi % 4]],
                           label=f"ge{t_}_{tthalf}")
                        gi += 1
                emit_wd2(2, grp, _wdk)
                iu = _it[0]
                _it[0] += 1
                emit_load(iu + 2)
                gi = 0
                for t_ in stiles:
                    col = (t_ % 2) * 128
                    k_ = t_ - grp * 11
                    for tthalf in range(2):
                        bku = 2 + gi % 2

                        def ummf(t, b=bku, col=col, th=tthalf, bi=iu % 3):
                            last = None
                            for c in range(NCH):
                                last = t.matmul(
                                    bank(b),
                                    wchunk(bi, c)[:, col:col + 128],
                                    h2F[:, c * 1024 + th * 512:c * 1024 + (th + 1) * 512],
                                    start=(c == 0), stop=(c == NCH - 1))
                            return last
                        urd = wreads(iu % 3) + [
                            h2F[:, c * 1024 + tthalf * 512:c * 1024 + (tthalf + 1) * 512]
                            for c in range(NCH)]
                        OP("pe", ummf, reads=urd,
                           writes=[bank(bku)], label=f"u{t_}_{tthalf}")
                        adst = aF[:, k_ * 1024 + tthalf * 512:k_ * 1024 + (tthalf + 1) * 512]
                        OP("act", lambda a, b=bku, d=adst, hh2=t_: a.activation(
                            d, bank(b), AF.Identity, scale=SUP.ap()[:, hh2:hh2 + 1]),
                           reads=[bank(bku), SUP.ap()], writes=[adst],
                           label=f"ue{t_}_{tthalf}")
                        OP("dve", lambda v, d=adst, g=g_hold[gi % 4]: v.tensor_mul(
                            d, g, d),
                           reads=[g_hold[gi % 4], adst], writes=[adst],
                           label=f"a{t_}_{tthalf}")
                        gi += 1
            emit_wd2(11, grp, _wdk)
            for tt in range(NO):
                for dm in range(4):
                    bkd = 4 + (dm % 2)

                    def dmm(t, b=bkd, tt=tt, dm=dm):
                        last = None
                        for k in range(11):
                            last = t.matmul(
                                bank(b),
                                aF[:, k * 1024 + tt * 128:k * 1024 + (tt + 1) * 128],
                                wd_tile(k)[:, dm * 512:(dm + 1) * 512],
                                start=(k == 0), stop=(k == 10))
                        return last
                    rdl = [aF] + [wd_tile(k) for k in range(11)]
                    OP("pe", dmm, reads=rdl, writes=[bank(bkd)],
                       label=f"d{grp}_{tt}_{dm}")
                    xv = xa[:, tt * 2048 + dm * 512:tt * 2048 + (dm + 1) * 512]
                    OP("dve", lambda v, x=xv, b=bkd: v.tensor_tensor(
                        x, x, bank(b), ALU.add),
                       reads=[xv, bank(bkd)], writes=[xv], label=f"de{grp}_{tt}_{dm}")

        for tt in range(NO):
            src = xa[:, tt * 2048:(tt + 1) * 2048]
            OP("sp", lambda s, sr=src, tt=tt: s.dma_start(
                out=y_out.ap()[tt * 128:(tt + 1) * 128, :], in_=sr),
               reads=[src], writes=[], label=f"st{tt}")
        for i in range(8):
            se = sch.sems[f"dmas{i}"]
            if se.n:
                sch.raw("sp", lambda s, h=se.h, v=se.n: s.wait_ge(h, v))

        block = E(nc.Block())
        sch.emit(block)

    return nc



def _mask_host(parity):
    d0 = parity * 128
    jj = np.arange(768)[None, :]
    row = np.arange(128)[:, None]
    m = np.where(jj <= row + d0 + 256, 0.0, NEG).astype(np.float32)
    return m.astype(ml_dtypes.bfloat16)


def _shared_weights(inp):
    wq = np.asarray(inp["L0_attn_Wq"])
    wk = np.asarray(inp["L0_attn_Wk"])
    wv = np.asarray(inp["L0_attn_Wv"])
    wo = np.asarray(inp["L0_attn_Wo"])
    wg = np.asarray(inp["L0_ffn_wq_gate"])
    wu = np.asarray(inp["L0_ffn_wq_up"])
    wd = np.asarray(inp["L0_ffn_wq_down"])

    def qkv_pp(w):
        return np.ascontiguousarray(
            w.reshape(8, 256, 16, 128).transpose(3, 0, 2, 1).reshape(128, 32768))

    def gu_pp(w):
        return np.ascontiguousarray(
            w.reshape(22, 256, 16, 128).transpose(3, 0, 2, 1).reshape(128, 90112))

    return {
        "wqp": qkv_pp(wq),
        "wkp": qkv_pp(wk),
        "wvp": qkv_pp(wv),
        "wop": np.ascontiguousarray(
            wo.reshape(2048, 16, 128).transpose(2, 1, 0).reshape(128, 32768)),
        "wgp": gu_pp(wg),
        "wup": gu_pp(wu),
        "wdp": np.ascontiguousarray(
            wd.reshape(2048, 44, 128).transpose(2, 1, 0).reshape(128, 90112)),
        "bq_pp": np.ascontiguousarray(np.asarray(inp["L0_attn_bq"]).reshape(16, 128).T),
        "bk_pp": np.ascontiguousarray(np.asarray(inp["L0_attn_bk"]).reshape(16, 128).T),
        "bv_pp": np.ascontiguousarray(np.asarray(inp["L0_attn_bv"]).reshape(16, 128).T),
        "bo_row": np.ascontiguousarray(np.asarray(inp["L0_attn_bo"]).reshape(1, 2048)),
        "sg_pp": np.ascontiguousarray(np.asarray(inp["L0_ffn_s_gate"]).reshape(44, 128).T),
        "su_pp": np.ascontiguousarray(np.asarray(inp["L0_ffn_s_up"]).reshape(44, 128).T),
        "sdb": np.ascontiguousarray(
            np.broadcast_to(np.asarray(inp["L0_ffn_s_down"]), (128, 2048))),
        "ident": np.eye(128, dtype=np.float32).astype(ml_dtypes.bfloat16),
    }


def _core_inputs(c, inp, shared):
    b, p = c // 2, c % 2
    X = np.asarray(inp["L0_attn_X"])
    rows = np.concatenate([np.arange((p + 2 * j) * 128, (p + 2 * j) * 128 + 128)
                           for j in range(8)])
    d = dict(shared)
    d["x"] = np.ascontiguousarray(X[b])
    d["xown"] = np.ascontiguousarray(X[b][rows])
    d["mask"] = _mask_host(p)
    return d, b, rows


def kernel(**inputs):
    nc = build_nc()
    shared = _shared_weights(inputs)
    in_maps = []
    meta = []
    for c in range(8):
        m, b, rows = _core_inputs(c, inputs, shared)
        in_maps.append(m)
        meta.append((b, rows))
    res = run_bass_kernel_spmd(nc, in_maps, core_ids=list(range(8)))
    out = np.empty((4, 2048, 2048), dtype=np.float32)
    for c in range(8):
        b, rows = meta[c]
        out[b][rows] = np.asarray(res.results[c]["y"], dtype=np.float32)
    return out
